# version 04
# baseline (speedup 1.0000x reference)
import numpy as np
import concourse.bass as bass
import concourse.mybir as mybir
from concourse.bass_utils import run_bass_kernel_spmd

F32 = mybir.dt.float32
BF16 = mybir.dt.bfloat16
I32 = mybir.dt.int32
AF = mybir.ActivationFunctionType
ALU = mybir.AluOpType

S = 4096
D = 1024
DH = 64
HL = 8
FQ = 512
NKC = 32
NCH = HL * NKC
TM = 2048
FF = 4096
CLIPV = 1.0 - 1e-6
HPI1 = 1.0 + np.pi / 2.0
NWO = 16
NWT = 144
NSMALL = 12


def _w(eng, sem, v):
    if v >= 1:
        eng.wait_ge(sem, v)


def build_nc(upto="full"):
    NPF = 1 if upto.startswith("ffn") else 4
    FSB = {"ffnb1": 1, "ffnb2a": 15, "ffnb2x": 16, "ffnb2y": 17, "ffnb2": 2, "ffnb3": 3}.get(upto, 99)
    FST = {"ffn0": 0, "ffn0a": 0, "ffn0b": 0, "ffn0d": -1, "ffn0e": -1, "ffn0f": -1, "ffn0g": -1,
           "ffn0h": 0, "ffn0i": 0, "ffn0m": -1, "ffn0n": -1, "ffn0p": -1, "ffn0q": -1, "ffn0r": -1, "ffn0u": -1, "ffn0u2": -1, "ffn0u3": -1, "ffna": 1, "ffnb1": 2, "ffnb2": 2, "ffnb2a": 2, "ffnb2x": 2, "ffnb2y": 2, "ffnb3": 2, "ffnb": 2, "ffnc": 3, "ffnd": 4}.get(upto, 9)
    NTT = 1 if upto in ("ffn0h", "ffn0i") else 4
    nc = bass.Bass("TRN2", target_bir_lowering=False, debug=False)

    x_in = nc.dram_tensor("x_in", [S, D], F32, kind="ExternalInput").ap()
    wq_in = nc.dram_tensor("wq_in", [D, FQ], F32, kind="ExternalInput").ap()
    wk_in = nc.dram_tensor("wk_in", [D, FQ], F32, kind="ExternalInput").ap()
    wv_in = nc.dram_tensor("wv_in", [D, FQ], F32, kind="ExternalInput").ap()
    bq_in = nc.dram_tensor("bq_in", [128, 4], F32, kind="ExternalInput").ap()
    bk_in = nc.dram_tensor("bk_in", [128, 4], F32, kind="ExternalInput").ap()
    bv_in = nc.dram_tensor("bv_in", [1, FQ], F32, kind="ExternalInput").ap()
    wo_in = nc.dram_tensor("wo_in", [D, D], F32, kind="ExternalInput").ap()
    bo_in = nc.dram_tensor("bo_in", [128, 8], F32, kind="ExternalInput").ap()
    wfc_in = nc.dram_tensor("wfc_in", [D, FF], F32, kind="ExternalInput").ap()
    wpj_in = nc.dram_tensor("wpj_in", [FF, D], F32, kind="ExternalInput").ap()
    id_in = nc.dram_tensor("id_in", [128, 128], F32, kind="ExternalInput").ap()
    eye2_in = nc.dram_tensor("eye2_in", [128, 2], F32, kind="ExternalInput").ap()
    blk2_in = nc.dram_tensor("blk2_in", [2, 128], F32, kind="ExternalInput").ap()
    sidx_in = nc.dram_tensor("sidx_in", [64, 8], I32, kind="ExternalInput").ap()
    gidx_in = nc.dram_tensor("gidx_in", [128, 4], I32, kind="ExternalInput").ap()
    fme_in = nc.dram_tensor("fme_in", [2, 1], I32, kind="ExternalInput").ap()
    fpr_in = nc.dram_tensor("fpr_in", [2, 1], I32, kind="ExternalInput").ap()
    non_in = nc.dram_tensor("non_in", [2, 16], I32, kind="ExternalInput").ap()
    out_o = nc.dram_tensor("out_o", [TM, D], F32, kind="ExternalOutput").ap()
    dbg_q = nc.dram_tensor("dbg_q", [FQ, S], BF16, kind="ExternalOutput").ap()
    dbg_k = nc.dram_tensor("dbg_k", [FQ, S], BF16, kind="ExternalOutput").ap()
    dbg_a = nc.dram_tensor("dbg_a", [128, 4 * S], BF16, kind="ExternalOutput").ap()
    dbg_h = nc.dram_tensor("dbg_h", [128, S], F32, kind="ExternalOutput").ap()
    dbg_u = nc.dram_tensor("dbg_u", [128, S], F32, kind="ExternalOutput").ap()
    dbg_s = nc.dram_tensor("dbg_s", [128, S], BF16, kind="ExternalOutput").ap()
    dbg_n = nc.dram_tensor("dbg_n", [128, 4], F32, kind="ExternalOutput").ap()

    qs = nc.dram_tensor("qs", [FQ, S], BF16, kind="Internal")
    ks = nc.dram_tensor("ks", [FQ, S], BF16, kind="Internal")
    vts = nc.dram_tensor("vts", [NKC, 128, FQ], BF16, kind="Internal")
    wfcb = nc.dram_tensor("wfcb", [D, FF], BF16, kind="Internal")
    wpjb = nc.dram_tensor("wpjb", [FF, D], BF16, kind="Internal")
    shd = nc.dram_tensor("shd", [8 * FQ, TM], BF16, kind="Internal", addr_space="Shared")
    shf = nc.dram_tensor("shf", [16, 16], I32, kind="Internal", addr_space="Shared")

    A = nc.alloc_sbuf_tensor
    consts = A("consts", [128, 8], F32)
    ident = A("ident", [128, 128], F32)
    eye2 = A("eye2", [128, 2], F32)
    blk2 = A("blk2", [2, 128], F32)
    onesrow = A("onesrow", [1, 128], F32)
    onescol = A("onescol", [128, 1], F32)
    ones2 = A("ones2", [128, 2], F32)
    bq_sb = A("bq_sb", [128, 4], F32)
    bk_sb = A("bk_sb", [128, 4], F32)
    bo_sb = A("bo_sb", [128, 8], F32)
    bvrow = A("bvrow", [1, FQ], F32)
    bvbc = A("bvbc", [128, FQ], F32)
    sidx = A("sidx", [64, 8], I32)
    gidx = A("gidx", [128, 4], I32)
    fme = A("fme", [2, 1], I32)
    fpr = A("fpr", [2, 1], I32)
    nonce = A("nonce", [2, 16], I32)
    fval = A("fval", [2, 16], I32)
    wo_sb = A("wo_sb", [128, 8 * D], BF16)

    used = nc.SBUF_PARTITION_SIZE_BYTES - nc.sbuf_bytes_remaining
    AB = ((used + 255) // 256) * 256 + 256
    CAP = 229344

    def ar(name, shape, dtype, off):
        nbytes = shape[-1] * mybir.dt.size(dtype)
        assert AB + off + nbytes <= CAP, (name, AB, off, nbytes)
        return nc.alloc_sbuf_tensor_at(name, shape, dtype, offset=AB + off)

    attn_all = ar("attn_all", [128, 4 * S], BF16, 0)

    p_xst = [ar("p_xst0", [128, 4096], F32, 0), ar("p_xst1", [128, 4096], F32, 16384)]
    p_xhat = ar("p_xhat", [128, 4096], F32, 32768)
    p_xhT = ar("p_xhT", [128, 4096], BF16, 49152)
    p_wqsb = ar("p_wqsb", [128, 4096], BF16, 57344)
    p_wksb = ar("p_wksb", [128, 4096], BF16, 65536)
    p_wvsb = ar("p_wvsb", [128, 4096], BF16, 73728)
    p_wst = [ar("p_wst0", [128, 512], F32, 81920), ar("p_wst1", [128, 512], F32, 83968)]
    p_ybuf = ar("p_ybuf", [128, 512], F32, 86016)
    p_ysq = ar("p_ysq", [128, 512], F32, 88064)
    p_stage = [ar("p_stage0", [128, 512], BF16, 90112), ar("p_stage1", [128, 512], BF16, 91136)]
    p_stat = ar("p_stat", [128, 8], F32, 92160)
    p_nrm = ar("p_nrm", [2, 512], F32, 92416)
    p_inv = ar("p_inv", [2, 512], F32, 94464)
    p_sqd = ar("p_sqd", [128, 1024], F32, 96512)

    a_qh = [ar("a_qh0", [64, S], BF16, 32768), ar("a_qh1", [64, S], BF16, 40960)]
    a_kh = [ar("a_kh0", [64, S], BF16, 49152), ar("a_kh1", [64, S], BF16, 57344)]
    a_vt = [ar("a_vt0", [128, NKC * DH], BF16, 65536), ar("a_vt1", [128, NKC * DH], BF16, 69632)]
    a_B0 = [ar("a_B00", [128, S], F32, 73728), ar("a_B01", [128, S], F32, 90112)]
    a_B1 = ar("a_B1", [128, S], F32, 106496)
    a_B2 = ar("a_B2", [128, S], F32, 122880)
    a_s = ar("a_s", [128, S], BF16, 139264)
    a_hacc = ar("a_hacc", [128, S], F32, 147456)
    a_vtt = [ar("a_vtt0", [128, 66], BF16, 163840), ar("a_vtt1", [128, 66], BF16, 164096)]
    a_ncc = ar("a_ncc", [128, 4], F32, 164352)
    a_nr = ar("a_nr", [1, 512], F32, 164608)
    a_tq = ar("a_tq", [1, 512], F32, 166656)
    a_sa = ar("a_sa", [1, 512], F32, 168704)
    a_sb = ar("a_sb", [1, 512], F32, 170752)
    a_sc = ar("a_sc", [1, 512], F32, 172800)
    a_sd = ar("a_sd", [1, 512], F32, 174848)
    a_wc = [ar("a_wc0", [128, 512], F32, 176896), ar("a_wc1", [128, 512], F32, 178944)]
    a_wb = [ar("a_wb0", [128, 512], BF16, 180992), ar("a_wb1", [128, 512], BF16, 182016)]

    f_x = ar("f_x", [128, 4096], F32, 32768)
    f_xfm = ar("f_xfm", [128, 4096], F32, 49152)
    f_x1p = ar("f_x1p", [128, 4096], F32, 65536)
    f_xh2 = ar("f_xh2", [128, 4096], BF16, 81920)
    f_xsq = ar("f_xsq", [128, 512], F32, 90112)
    f_h = ar("f_h", [128, 16384], BF16, 92160)
    f_w1 = [ar("f_w10", [128, 1024], BF16, 124928), ar("f_w11", [128, 1024], BF16, 126976)]
    f_w2 = [ar("f_w20", [128, 4096], BF16, 129024), ar("f_w21", [128, 4096], BF16, 137216)]
    f_pap = [ar("f_pap0", [128, 2048], BF16, 145408), ar("f_pap1", [128, 2048], BF16, 149504)]
    f_y = ar("f_y", [128, 512], F32, 153600)
    f_mA = ar("f_mA", [128, 512], F32, 155648)
    f_mB = ar("f_mB", [128, 512], F32, 157696)
    f_mC = ar("f_mC", [128, 512], F32, 159744)
    f_spk = ar("f_spk", [128, 512], F32, 161792)
    f_acc = ar("f_acc", [128, 512], F32, 163840)
    f_hr = ar("f_hr", [128, 512], F32, 165888)
    f_ofm = ar("f_ofm", [128, 512], F32, 167936)
    f_os = [ar("f_os0", [128, 128], F32, 169984), ar("f_os1", [128, 128], F32, 170496),
            ar("f_os2", [128, 128], F32, 171008), ar("f_os3", [128, 128], F32, 171520)]
    f_st = ar("f_st", [2, 512], F32, 178176)
    f_st2 = ar("f_st2", [2, 512], F32, 180224)
    f_st3 = ar("f_st3", [1, 512], F32, 182272)
    f_st4 = ar("f_st4", [1, 512], F32, 184320)

    psA = nc.alloc_psum_tensor("psA", [128, 1024], F32)
    psB = nc.alloc_psum_tensor("psB", [128, 1024], F32)
    pv0 = nc.alloc_psum_tensor("pv0", [128, 512], F32)
    pv1 = nc.alloc_psum_tensor("pv1", [128, 512], F32)
    mm0 = nc.alloc_psum_tensor("mm0", [128, 512], F32)
    mm1 = nc.alloc_psum_tensor("mm1", [128, 512], F32)
    PVB = [pv0, pv1]
    MM = [mm0, mm1]

    semnames = [
        "s_small", "s_dve", "s_bv",
        "p0wld0", "p0wld1", "p0wc", "p0x0", "p0x1", "p0sq", "p0var", "p0rstd", "p0xh",
        "p0T", "p0Tev", "p0mm", "p0yb", "p0ysq", "p0nrmm", "p0sqr", "p0inv",
        "p0bc", "p0fin", "p0st", "p0sd0", "p0sd1",
        "bar",
        "hload0", "hload1", "dot", "clip", "lnS", "dS", "sS", "cS", "vtS", "pvS", "evS",
        "stg", "ep1", "scl", "bcS", "attS",
        "wcld0", "wcld1", "wcdn", "wcod0", "wcod1", "wcst",
        "exs", "pls", "gat0", "gat1",
        "fxld", "fxT", "fxfm", "fop", "fx1", "fstq", "fsq2", "fstm", "fvar",
        "frst", "fstc", "fbc", "fxh2", "fw1a", "fw1b", "fhmm", "fhr", "fhq", "fw2a", "fw2b", "fyS",
        "fspk", "fof", "foT", "foev", "fout0", "fout1", "vch", "sch",
    ]
    from contextlib import ExitStack
    stack = ExitStack()
    sems = {n: stack.enter_context(nc.semaphore(n)) for n in semnames}
    G = sems.__getitem__

    cast_tiles = []
    for i in range(NWO):
        r, c = i // 2, i % 2
        cast_tiles.append(("wo", r, c, wo_in[r * 128:(r + 1) * 128, c * 512:(c + 1) * 512]))
    for j in range(64):
        r, c = j // 8, j % 8
        cast_tiles.append(("wfc", r, c, wfc_in[r * 128:(r + 1) * 128, c * 512:(c + 1) * 512]))
    for j in range(64):
        r, c = j // 2, j % 2
        cast_tiles.append(("wpj", r, c, wpj_in[r * 128:(r + 1) * 128, c * 512:(c + 1) * 512]))

    block = stack.enter_context(nc.Block())

    @block.tensor
    def _(t):
        t.wait_ge(G("s_small"), 16 * NSMALL)
        t.wait_ge(G("s_dve"), 1)
        t.matmul(mm0[:, 0:512], onesrow[0:1, :], bvrow[0:1, :], start=True, stop=True).then_inc(G("p0bc"), 1)

        for pp in range(8):
            _w(t, G("p0st"), 12 * pp)
            for tt in range(4):
                gi = 4 * pp + tt
                _w(t, G("p0xh"), gi + 1)
                for j in range(8):
                    i = 8 * gi + j
                    _w(t, G("p0Tev"), i - 1)
                    t.transpose(MM[i % 2][:, 0:128],
                                p_xhat[:, tt * 1024 + j * 128: tt * 1024 + (j + 1) * 128],
                                ident[:, :]).then_inc(G("p0T"), 1)
            if pp == 0:
                t.wait_ge(G("p0wc"), 24)
            _w(t, G("p0Tev"), 32 * (pp + 1))
            for gr in range(12):
                gg = 12 * pp + gr
                pv = PVB[gr % 2]
                gp2 = gg - 2
                if gp2 >= 0:
                    p2, g2 = divmod(gp2, 12)
                    if g2 < 8:
                        _w(t, G("p0yb"), 8 * p2 + g2 + 1)
                    else:
                        _w(t, G("p0st"), gp2 + 1)
                if gr < 8:
                    w_sb = p_wqsb if gr < 4 else p_wksb
                    fc = gr % 4
                    for dc in range(8):
                        ins = t.matmul(pv[:, 0:512],
                                       w_sb[:, dc * 512 + fc * 128: dc * 512 + (fc + 1) * 128],
                                       p_xhT[:, dc * 512:(dc + 1) * 512],
                                       start=(dc == 0), stop=(dc == 7))
                else:
                    tc = gr - 8
                    for dc in range(8):
                        ins = t.matmul(pv[:, 0:512],
                                       p_xhT[:, dc * 512 + tc * 128: dc * 512 + (tc + 1) * 128],
                                       p_wvsb[:, dc * 512:(dc + 1) * 512],
                                       start=(dc == 0), stop=(dc == 7))
                ins.then_inc(G("p0mm"), 1)
                for grn in ([gr - 1] if 1 <= gr <= 8 else []):
                    gg2 = 8 * pp + grn
                    mm = MM[grn % 2]
                    _w(t, G("p0ysq"), gg2 + 1)
                    _w(t, G("p0st"), 12 * pp + grn - 1)
                    t.matmul(mm[0:2, 0:512], eye2[:, :], p_ysq[:, 0:512],
                             start=True, stop=True).then_inc(G("p0nrmm"), 1)
                    _w(t, G("p0inv"), gg2 + 1)
                    t.matmul(mm[:, 0:512], blk2[:, :], p_inv[0:2, 0:512],
                             start=True, stop=True).then_inc(G("p0bc"), 1)

        t.wait_ge(G("p0bc"), 65)
        t.sem_inc(G("bar"), 1)
        t.wait_ge(G("bar"), 5)
        if upto == "p0":
            return

        def emit_pv(kk):
            _w(t, G("vtS"), kk + 1)
            for p in range(8):
                pvb = PVB[(8 * kk + p) % 2]
                _w(t, G("evS"), 8 * kk + p - 1)
                t.matmul(pvb[0:66, 0:512], a_vtt[kk % 2][:, :],
                         a_s[:, p * 512:(p + 1) * 512],
                         start=True, stop=True).then_inc(G("pvS"), 1)

        def emit_epi_bc(h):
            for ppn in range(8):
                e = 8 * h + ppn
                _w(t, G("scl"), e + 1)
                _w(t, G("attS"), e - 1)
                t.matmul(MM[ppn % 2][0:64, 0:512], onesrow[0:1, 0:64], a_sd[0:1, :],
                         start=True, stop=True).then_inc(G("bcS"), 1)

        for kk in range(NCH):
            h, kc = divmod(kk, NKC)
            if kc == 0:
                t.wait_ge(G("hload0" if h % 2 == 0 else "hload1"), 48 * (h // 2 + 1))
            for q in range(4):
                Q = 4 * kk + q
                _w(t, G("clip"), Q - 1)
                pb = psA if q % 2 == 0 else psB
                t.matmul(pb[:, 0:512], a_kh[h % 2][0:64, kc * 128:(kc + 1) * 128],
                         a_qh[h % 2][0:64, q * 1024: q * 1024 + 512],
                         start=True, stop=True)
                t.matmul(pb[:, 512:1024], a_kh[h % 2][0:64, kc * 128:(kc + 1) * 128],
                         a_qh[h % 2][0:64, q * 1024 + 512: (q + 1) * 1024],
                         start=True, stop=True).then_inc(G("dot"), 1)
            if kk > 0:
                emit_pv(kk - 1)
            if kc == 0 and kk > 0:
                emit_epi_bc(h - 1)
        emit_pv(NCH - 1)
        emit_epi_bc(HL - 1)

        t.wait_ge(G("bcS"), 64)
        t.wait_ge(G("pvS"), 8 * NCH)
        t.sem_inc(G("bar"), 1)
        if upto in ("attn", "exch"):
            return
        t.wait_ge(G("bar"), 10)
        if upto == "ffn0m":
            t.wait_ge(G("wcst"), 1)
            return
        if upto in ("ffn0p", "ffn0q", "ffn0r", "ffn0u", "ffn0u2", "ffn0u3"):
            nT = {"ffn0p": 1, "ffn0q": 2}.get(upto, 8)
            HS = upto.startswith("ffn0u")
            t.wait_ge(G("wcst"), 1)
            t.wait_ge(G("fxld"), 16)
            for i in range(nT):
                if HS:
                    _w(t, G("fxfm"), i - 1)
                t.transpose(MM[i % 2][:, 0:128], f_x[:, i * 128:(i + 1) * 128],
                            ident[:, :]).then_inc(G("fxT"), 1)
            return
        if upto == "ffn0n":
            t.wait_ge(G("wcdn"), NWT)
            return
        if upto == "ffn0g":
            t.wait_ge(G("fxld"), 16)
            t.transpose(MM[0][:, 0:128], f_x[:, 0:128], ident[:, :])
            return
        if upto in ("gat", "ffn0d", "ffn0e", "ffn0f"):
            return

        t.wait_ge(G("wcst"), 1)

        def emit_oT(i):
            for tc in range(4):
                j = 4 * i + tc
                _w(t, G("foev"), j - 1)
                t.transpose(MM[j % 2][:, 0:128], f_ofm[:, tc * 128:(tc + 1) * 128],
                            ident[:, :]).then_inc(G("foT"), 1)

        for pf in range(NPF):
            t.wait_ge(G("fxld"), 16 * (pf + 1))
            for tt in range(NTT):
                for j in range(8):
                    i = 32 * pf + 8 * tt + j
                    _w(t, G("fxfm"), i - 1)
                    t.transpose(MM[i % 2][:, 0:128],
                                f_x[:, tt * 1024 + j * 128: tt * 1024 + (j + 1) * 128],
                                ident[:, :]).then_inc(G("fxT"), 1)
            if FST <= 0:
                continue
            t.wait_ge(G("gat0" if pf % 2 == 0 else "gat1"), 64 * (pf // 2 + 1))
            t.wait_ge(G("fxfm"), 32 * (pf + 1))
            for dc in range(8):
                i = 8 * pf + dc
                _w(t, G("fx1"), i - 1)
                for fc in range(8):
                    if fc < 4:
                        rhs = attn_all[:, fc * S + pf * 512: fc * S + pf * 512 + 512]
                    else:
                        rhs = f_pap[pf % 2][:, (fc - 4) * 512:(fc - 3) * 512]
                    ins = t.matmul(MM[dc % 2][:, 0:512],
                                   wo_sb[:, fc * D + dc * 128: fc * D + (dc + 1) * 128],
                                   rhs, start=(fc == 0), stop=(fc == 7))
                ins.then_inc(G("fop"), 1)
            if FST <= 1:
                continue
            for dc in range(8):
                i = 8 * pf + dc
                t.wait_ge(G("fstq"), i + 1)
                t.matmul(pv1[0:2, 0:512], ones2[:, :], f_xsq[:, 0:512],
                         start=(dc == 0), stop=(dc == 7)).then_inc(G("fsq2"), 1)
                t.wait_ge(G("fx1"), i + 1)
                ins = t.matmul(pv0[0:2, 0:512], ones2[:, :], f_x1p[:, dc * 512:(dc + 1) * 512],
                               start=(dc == 0), stop=(dc == 7))
            ins.then_inc(G("fstm"), 1)
            if FSB <= 2:
                continue
            t.wait_ge(G("frst"), pf + 1)
            t.matmul(pv0[:, 0:512], onesrow[0:1, :], f_st[0:1, :], start=True, stop=True)
            t.matmul(pv1[:, 0:512], onesrow[0:1, :], f_st3[0:1, :],
                     start=True, stop=True).then_inc(G("fbc"), 1)
            if FST <= 2:
                continue
            t.wait_ge(G("fxh2"), 16 * (pf + 1))
            for fc in range(32):
                i = 32 * pf + fc
                t.wait_ge(G("fw1a" if i % 2 == 0 else "fw1b"), 16 * (i // 2 + 1))
                _w(t, G("fhr"), i - 1)
                for dc in range(8):
                    ins = t.matmul(MM[fc % 2][:, 0:512],
                                   f_w1[i % 2][:, dc * 128:(dc + 1) * 128],
                                   f_xh2[:, dc * 512:(dc + 1) * 512],
                                   start=(dc == 0), stop=(dc == 7))
                ins.then_inc(G("fhmm"), 1)
            if FST <= 3:
                continue
            t.wait_ge(G("fhq"), 32 * (pf + 1))
            for dc in range(8):
                i = 8 * pf + dc
                t.wait_ge(G("fw2a" if i % 2 == 0 else "fw2b"), 16 * (i // 2 + 1))
                _w(t, G("fspk"), i - 1)
                for fc in range(32):
                    ins = t.matmul(PVB[dc % 2][:, 0:512],
                                   f_w2[i % 2][:, fc * 128:(fc + 1) * 128],
                                   f_h[:, fc * 512:(fc + 1) * 512],
                                   start=(fc == 0), stop=(fc == 31))
                ins.then_inc(G("fyS"), 1)
                if FST <= 4:
                    continue
                if dc > 0:
                    _w(t, G("fof"), i)
                    emit_oT(i - 1)
            if FST > 4:
                _w(t, G("fof"), 8 * pf + 8)
                emit_oT(8 * pf + 7)

    @block.scalar
    def _(sc):
        scn = [0]
        slast = [None]

        def sstep(ins, sem=None, val=None):
            if sem is None:
                ins.then_inc(G("sch"), 1)
                scn[0] += 1
                slast[0] = ("sch", scn[0])
            else:
                slast[0] = (sem, val)

        def swait():
            if slast[0] is not None:
                _w(sc, G(slast[0][0]), slast[0][1])

        c_tanh = consts[:, 0:1]
        c_hpi = consts[:, 1:2]
        c_m65 = consts[:, 2:3]
        c_mh = consts[:, 3:4]
        c_eps = consts[:, 4:5]
        c_p80 = consts[:, 5:6]
        c_b40 = consts[:, 6:7]

        sc.wait_ge(G("s_dve"), 1)
        sc.wait_ge(G("s_small"), 16 * NSMALL)

        for pp in range(8):
            xst = p_xst[pp % 2]
            for tt in range(4):
                gi = 4 * pp + tt
                sc.wait_ge(G("p0x0" if pp % 2 == 0 else "p0x1"), 16 * (pp // 2 + 1))
                _w(sc, G("p0var"), gi)
                sc.activation(out=p_sqd[:, :], in_=xst[:, tt * 1024:(tt + 1) * 1024],
                              func=AF.Square, accum_out=p_stat[:, 1:2]).then_inc(G("p0sq"), 1)
                _w(sc, G("p0var"), gi + 1)
                _w(sc, G("p0xh"), gi)
                swait()
                sstep(sc.activation(out=p_stat[:, 5:6], in_=p_stat[:, 6:7], func=AF.Ln,
                                    bias=c_eps, scale=1.0))
                swait()
                sstep(sc.activation(out=p_stat[:, 7:8], in_=p_stat[:, 5:6], func=AF.Exp,
                              scale=c_mh).then_inc(G("p0rstd"), 1), "p0rstd", gi + 1)
            for gr in range(8):
                gg = 8 * pp + gr
                _w(sc, G("p0yb"), 8 * pp + gr + 1)
                sc.activation(out=p_ysq[:, :], in_=p_ybuf[:, :],
                              func=AF.Square).then_inc(G("p0ysq"), 1)
                _w(sc, G("p0nrmm"), gg + 1)
                sc.activation(out=p_nrm[0:2, :], in_=MM[gr % 2][0:2, 0:512],
                              func=AF.Sqrt).then_inc(G("p0sqr"), 1)

        sc.wait_ge(G("p0sqr"), 64)
        sc.sem_inc(G("bar"), 1)
        sc.wait_ge(G("bar"), 5)
        if upto == "p0":
            return

        for kk in range(NCH):
            h, kc = divmod(kk, NKC)
            B0 = a_B0[kk % 2]
            sc.wait_ge(G("clip"), 4 * kk + 4)
            _w(sc, G("dS"), kk)
            swait()
            sstep(sc.activation(out=a_B1[:, :], in_=B0[:, :], func=AF.Ln, scale=-1.0, bias=1.0))
            sstep(sc.activation(out=a_B2[:, :], in_=B0[:, :], func=AF.Ln, scale=1.0,
                                bias=1.0).then_inc(G("lnS"), 1), "lnS", kk + 1)
            sc.wait_ge(G("dS"), kk + 1)
            sstep(sc.activation(out=a_B1[:, :], in_=B0[:, :], func=AF.Tanh, scale=c_tanh))
            swait()
            sstep(sc.activation(out=a_B2[:, :], in_=a_B1[:, :], func=AF.Arctan))
            swait()
            sstep(sc.activation(out=a_B1[:, :], in_=a_B2[:, :], func=AF.Ln, scale=-2.0, bias=c_hpi))
            _w(sc, G("pvS"), 8 * kk)
            swait()
            sstep(sc.activation(out=a_s[:, :], in_=a_B1[:, :], func=AF.Exp, scale=c_m65,
                                accum_out=a_ncc[:, 0:1]).then_inc(G("sS"), 1), "sS", kk + 1)
            _w(sc, G("vtS"), kk)
            swait()
            sstep(sc.activation(out=a_ncc[:, 1:2], in_=a_ncc[:, 0:1], func=AF.Ln,
                                scale=c_p80))
            swait()
            sstep(sc.activation(out=a_ncc[:, 2:3], in_=a_ncc[:, 1:2], func=AF.Exp,
                          scale=c_mh, bias=c_b40).then_inc(G("cS"), 1), "cS", kk + 1)
            if kc == NKC - 1:
                for ppn in range(8):
                    e = 8 * h + ppn
                    sc.wait_ge(G("stg"), 32 * (e + 1))
                    swait()
                    sstep(sc.activation(out=a_sa[0:1, :], in_=a_nr[0:1, :], func=AF.Ln,
                                        scale=c_p80[0:1, :]))
                    swait()
                    sstep(sc.activation(out=a_sb[0:1, :], in_=a_sa[0:1, :], func=AF.Exp,
                                  scale=c_mh[0:1, :], bias=c_b40[0:1, :]).then_inc(G("ep1"), 1), "ep1", e + 1)

        sc.wait_ge(G("ep1"), 64)
        sc.wait_ge(G("cS"), NCH)
        sc.sem_inc(G("bar"), 1)
        if upto in ("attn", "exch"):
            return
        sc.wait_ge(G("bar"), 10)
        if upto in ("gat", "ffn0d", "ffn0e", "ffn0f", "ffn0g", "ffn0m", "ffn0n", "ffn0p", "ffn0q", "ffn0r", "ffn0u", "ffn0u2", "ffn0u3"):
            return

        if FST <= 1:
            return
        for pf in range(NPF):
            for dc in range(8):
                i = 8 * pf + dc
                sc.wait_ge(G("fx1"), i + 1)
                _w(sc, G("fsq2"), i)
                sc.activation(out=f_xsq[:, :], in_=f_x1p[:, dc * 512:(dc + 1) * 512],
                              func=AF.Square).then_inc(G("fstq"), 1)
            if FSB <= 1 or FSB in (15, 16, 17):
                continue
            sc.wait_ge(G("fvar"), pf + 1)
            sstep(sc.activation(out=f_st2[0:1, :], in_=f_st4[0:1, :], func=AF.Ln,
                                bias=c_eps[0:1, :], scale=1.0))
            swait()
            sc.activation(out=f_st3[0:1, :], in_=f_st2[0:1, :], func=AF.Exp,
                          scale=c_mh[0:1, :]).then_inc(G("frst"), 1)
            if FST <= 2:
                continue
            for fc in range(32):
                i = 32 * pf + fc
                sc.wait_ge(G("fhr"), i + 1)
                if pf > 0 and fc == 0:
                    sc.wait_ge(G("fyS"), 8 * pf)
                sc.activation(out=f_h[:, fc * 512:(fc + 1) * 512], in_=f_hr[:, :],
                              func=AF.Square).then_inc(G("fhq"), 1)

    @block.vector
    def _(v):
        vcn = [0]
        vlast = [None]

        def vstep(ins, sem=None, val=None):
            if sem is None:
                ins.then_inc(G("vch"), 1)
                vcn[0] += 1
                vlast[0] = ("vch", vcn[0])
            else:
                vlast[0] = (sem, val)

        def vwait():
            if vlast[0] is not None:
                _w(v, G(vlast[0][0]), vlast[0][1])

        v.memset(consts[:, 0:1], 0.25)
        v.memset(consts[:, 1:2], HPI1)
        v.memset(consts[:, 2:3], -65.0)
        v.memset(consts[:, 3:4], -0.5)
        v.memset(consts[:, 4:5], 1e-5)
        v.memset(consts[:, 5:6], 1.2089258196146292e24)
        v.memset(consts[:, 6:7], 27.725887222397812)
        v.memset(onesrow[:, :], 1.0)
        v.memset(onescol[:, :], 1.0)
        v.memset(ones2[:, :], 1.0)
        v.memset(a_vtt[0][:, 65:66], 1.0)
        v.memset(a_vtt[1][:, 65:66], 1.0).then_inc(G("s_dve"), 1)
        v.wait_ge(G("p0bc"), 1)
        v.tensor_copy(out=bvbc[:, :], in_=mm0[:, 0:512]).then_inc(G("s_bv"), 1)

        for i in range(24):
            wsb = [p_wqsb, p_wksb, p_wvsb][i // 8]
            dc = i % 8
            v.wait_ge(G("p0wld0" if i % 2 == 0 else "p0wld1"), 16 * (i // 2 + 1))
            v.tensor_copy(out=wsb[:, dc * 512:(dc + 1) * 512],
                          in_=p_wst[i % 2][:, :]).then_inc(G("p0wc"), 1)
        for pp in range(8):
            xst = p_xst[pp % 2]
            _w(v, G("p0mm"), 12 * pp)
            for tt in range(4):
                gi = 4 * pp + tt
                v.wait_ge(G("p0x0" if pp % 2 == 0 else "p0x1"), 16 * (pp // 2 + 1))
                vwait()
                vstep(v.reduce_sum(out=p_stat[:, 0:1], in_=xst[:, tt * 1024:(tt + 1) * 1024],
                                   axis=mybir.AxisListType.X))
                v.wait_ge(G("p0sq"), gi + 1)
                vwait()
                vstep(v.tensor_scalar(out=p_stat[:, 0:1], in0=p_stat[:, 0:1],
                                      scalar1=1.0 / D, scalar2=None, op0=ALU.mult))
                vstep(v.tensor_scalar(out=p_stat[:, 1:2], in0=p_stat[:, 1:2],
                                      scalar1=1.0 / D, scalar2=None, op0=ALU.mult))
                vwait()
                vstep(v.tensor_tensor(out=p_stat[:, 2:3], in0=p_stat[:, 0:1],
                                      in1=p_stat[:, 0:1], op=ALU.mult))
                vwait()
                v.tensor_tensor(out=p_stat[:, 6:7], in0=p_stat[:, 1:2],
                                in1=p_stat[:, 2:3], op=ALU.subtract).then_inc(G("p0var"), 1)
                v.wait_ge(G("p0rstd"), gi + 1)
                if tt == 0:
                    _w(v, G("p0T"), 32 * pp)
                vstep(v.tensor_scalar(out=p_xhat[:, tt * 1024:(tt + 1) * 1024],
                                      in0=xst[:, tt * 1024:(tt + 1) * 1024],
                                      scalar1=p_stat[:, 0:1], scalar2=p_stat[:, 7:8],
                                      op0=ALU.subtract, op1=ALU.mult).then_inc(G("p0xh"), 1), "p0xh", gi + 1)
            for i8 in range(32):
                i = 32 * pp + i8
                tt = i8 // 8
                j = i8 % 8
                v.wait_ge(G("p0T"), i + 1)
                v.tensor_copy(out=p_xhT[:, j * 512 + tt * 128: j * 512 + (tt + 1) * 128],
                              in_=MM[i % 2][:, 0:128]).then_inc(G("p0Tev"), 1)
            for gr in range(12):
                gg = 12 * pp + gr
                si = gg
                stage = p_stage[si % 2]
                v.wait_ge(G("p0mm"), gg + 1)
                if gr < 8:
                    e = 8 * pp + gr
                    bias = (bq_sb if gr < 4 else bk_sb)[:, (gr % 4):(gr % 4) + 1]
                    vwait()
                    vstep(v.tensor_scalar(out=p_ybuf[:, :], in0=PVB[gr % 2][:, 0:512],
                                          scalar1=bias, scalar2=None,
                                          op0=ALU.add).then_inc(G("p0yb"), 1), "p0yb", 8 * pp + gr + 1)
                    v.wait_ge(G("p0sqr"), e + 1)
                    vstep(v.tensor_scalar(out=p_nrm[0:2, :], in0=p_nrm[0:2, :],
                                          scalar1=1e-12, scalar2=None, op0=ALU.max))
                    vwait()
                    v.reciprocal(out=p_inv[0:2, :], in_=p_nrm[0:2, :]).then_inc(G("p0inv"), 1)
                    v.wait_ge(G("p0bc"), e + 2)
                    if si >= 2:
                        v.wait_ge(G("p0sd0" if si % 2 == 0 else "p0sd1"), 16 * (si // 2))
                    vwait()
                    vstep(v.tensor_tensor(out=stage[:, :], in0=p_ybuf[:, :],
                                          in1=MM[gr % 2][:, 0:512],
                                          op=ALU.mult).then_inc(G("p0st"), 1), "p0st", si + 1)
                else:
                    v.wait_ge(G("s_bv"), 1)
                    if si >= 2:
                        v.wait_ge(G("p0sd0" if si % 2 == 0 else "p0sd1"), 16 * (si // 2))
                    vstep(v.tensor_tensor(out=stage[:, :], in0=PVB[gr % 2][:, 0:512],
                                          in1=bvbc[:, :],
                                          op=ALU.add).then_inc(G("p0st"), 1), "p0st", si + 1)

        v.wait_ge(G("p0st"), 96)
        v.sem_inc(G("bar"), 1)
        v.wait_ge(G("bar"), 5)
        if upto == "p0":
            return

        def emit_clip(kk):
            B0 = a_B0[kk % 2]
            _w(v, G("sS"), kk - 1)
            for q in range(4):
                Q = 4 * kk + q
                v.wait_ge(G("dot"), Q + 1)
                pb = psA if q % 2 == 0 else psB
                v.tensor_scalar(out=B0[:, q * 1024:(q + 1) * 1024], in0=pb[:, :],
                                scalar1=CLIPV, scalar2=-CLIPV,
                                op0=ALU.min, op1=ALU.max).then_inc(G("clip"), 1)

        emit_clip(0)
        for kk in range(NCH):
            h, kc = divmod(kk, NKC)
            B0 = a_B0[kk % 2]
            v.wait_ge(G("lnS"), kk + 1)
            v.tensor_tensor(out=B0[:, :], in0=a_B2[:, :], in1=a_B1[:, :],
                            op=ALU.subtract).then_inc(G("dS"), 1)
            if kk + 1 < NCH:
                emit_clip(kk + 1)
            if kk < NWT:
                i = kk
                v.wait_ge(G("wcld0" if i % 2 == 0 else "wcld1"), 16 * (i // 2 + 1))
                if i < NWO:
                    kind, r, c, _src = cast_tiles[i]
                    dst = wo_sb[:, r * 1024 + c * 512: r * 1024 + (c + 1) * 512]
                else:
                    io = i - NWO
                    if io >= 2:
                        v.wait_ge(G("wcod0" if io % 2 == 0 else "wcod1"), 16 * (io // 2))
                    dst = a_wb[i % 2][:, :]
                v.tensor_copy(out=dst, in_=a_wc[i % 2][:, :]).then_inc(G("wcdn"), 1)
                if i == NWO - 1:
                    v.sem_inc(G("wcst"), 1)
            v.wait_ge(G("cS"), kk + 1)
            v.wait_ge(G("hload0" if h % 2 == 0 else "hload1"), 48 * (h // 2 + 1))
            _w(v, G("pvS"), 8 * (kk - 1))
            vt_t = a_vtt[kk % 2]
            v.tensor_scalar(out=vt_t[:, 0:64], in0=a_vt[h % 2][:, kc * 64:(kc + 1) * 64],
                            scalar1=a_ncc[:, 2:3], scalar2=None, op0=ALU.mult)
            v.tensor_copy(out=vt_t[:, 64:65], in_=a_ncc[:, 2:3]).then_inc(G("vtS"), 1)
            for p in range(8):
                v.wait_ge(G("pvS"), 8 * kk + p + 1)
                if kc == 0 and p == 0:
                    _w(v, G("attS"), 8 * h)
                pvb = PVB[(8 * kk + p) % 2]
                if kc == 0:
                    ins = v.tensor_copy(out=a_hacc[0:66, p * 512:(p + 1) * 512],
                                        in_=pvb[0:66, 0:512])
                else:
                    ins = v.tensor_tensor(out=a_hacc[0:66, p * 512:(p + 1) * 512],
                                          in0=pvb[0:66, 0:512],
                                          in1=a_hacc[0:66, p * 512:(p + 1) * 512],
                                          op=ALU.add)
                ins.then_inc(G("evS"), 1)
            if kc == NKC - 1:
                for ppn in range(8):
                    e = 8 * h + ppn
                    v.wait_ge(G("ep1"), e + 1)
                    vstep(v.tensor_tensor(out=a_sc[0:1, :], in0=a_sb[0:1, :],
                                          in1=a_tq[0:1, :], op=ALU.mult))
                    vwait()
                    vstep(v.tensor_scalar(out=a_sd[0:1, :], in0=a_sc[0:1, :],
                                          scalar1=1e-12, scalar2=None, op0=ALU.max))
                    vwait()
                    vstep(v.reciprocal(out=a_sc[0:1, :], in_=a_sd[0:1, :]))
                    vwait()
                    v.tensor_tensor(out=a_sd[0:1, :], in0=a_sb[0:1, :],
                                    in1=a_sc[0:1, :], op=ALU.mult).then_inc(G("scl"), 1)
                    v.wait_ge(G("bcS"), e + 1)
                    row0 = 64 * (h % 2)
                    cbase = (h // 2) * S
                    v.tensor_tensor(out=attn_all[row0:row0 + 64,
                                                 cbase + ppn * 512: cbase + (ppn + 1) * 512],
                                    in0=a_hacc[0:64, ppn * 512:(ppn + 1) * 512],
                                    in1=MM[ppn % 2][0:64, 0:512],
                                    op=ALU.mult).then_inc(G("attS"), 1)

        v.wait_ge(G("attS"), 64)
        v.wait_ge(G("evS"), 8 * NCH)
        v.sem_inc(G("bar"), 1)
        if upto in ("attn", "exch"):
            return
        v.wait_ge(G("bar"), 10)
        if upto in ("ffn0p", "ffn0q", "ffn0r", "ffn0u", "ffn0u2", "ffn0u3"):
            nT = {"ffn0p": 1, "ffn0q": 2}.get(upto, 8)
            HS = upto.startswith("ffn0u")
            for i in range(nT):
                v.wait_ge(G("fxT"), i + 1)
                v.tensor_copy(out=f_xfm[:, i * 128:(i + 1) * 128],
                              in_=MM[i % 2][:, 0:128]).then_inc(G("fxfm"), 1)
            return
        if upto in ("ffn0m", "ffn0n"):
            return
        if upto == "ffn0g":
            v.wait_ge(G("fxld"), 16)
            v.tensor_copy(out=f_xfm[:, 0:128], in_=MM[0][:, 0:128])
            return
        if upto in ("gat", "ffn0b", "ffn0d", "ffn0e", "ffn0f"):
            return

        for pf in range(NPF):
            for i8 in range(8 * NTT):
                i = 32 * pf + i8
                tt = i8 // 8
                j = i8 % 8
                v.wait_ge(G("fxT"), i + 1)
                v.tensor_copy(out=f_xfm[:, j * 512 + tt * 128: j * 512 + (tt + 1) * 128],
                              in_=MM[i % 2][:, 0:128]).then_inc(G("fxfm"), 1)
            if FST <= 0:
                continue
            for dc in range(8):
                i = 8 * pf + dc
                v.wait_ge(G("fop"), i + 1)
                if dc == 0 and FST > 4:
                    _w(v, G("fof"), 8 * pf)
                v.scalar_tensor_tensor(out=f_x1p[:, dc * 512:(dc + 1) * 512],
                                       in0=MM[dc % 2][:, 0:512],
                                       scalar=bo_sb[:, dc:dc + 1],
                                       in1=f_xfm[:, dc * 512:(dc + 1) * 512],
                                       op0=ALU.add, op1=ALU.add).then_inc(G("fx1"), 1)
            if FST <= 1:
                continue
            if FSB <= 1:
                continue
            v.wait_ge(G("fstm"), pf + 1)
            vstep(v.tensor_scalar(out=f_st[0:1, :], in0=pv0[0:1, 0:512],
                                  scalar1=1.0 / D, scalar2=None, op0=ALU.mult))
            vstep(v.tensor_scalar(out=f_st2[0:1, :], in0=pv1[0:1, 0:512],
                                  scalar1=1.0 / D, scalar2=None, op0=ALU.mult))
            vwait()
            vstep(v.tensor_tensor(out=f_st3[0:1, :], in0=f_st[0:1, :], in1=f_st[0:1, :],
                                  op=ALU.mult))
            vwait()
            v.tensor_tensor(out=f_st4[0:1, :], in0=f_st2[0:1, :], in1=f_st3[0:1, :],
                            op=ALU.subtract).then_inc(G("fvar"), 1)
            if FSB <= 3:
                continue
            v.wait_ge(G("fbc"), pf + 1)
            for dc in range(8):
                if pf > 0 and dc == 0:
                    v.wait_ge(G("fhmm"), 32 * pf)
                vstep(v.tensor_tensor(out=f_y[:, :], in0=f_x1p[:, dc * 512:(dc + 1) * 512],
                                      in1=pv0[:, 0:512], op=ALU.subtract))
                vwait()
                v.tensor_tensor(out=f_xh2[:, dc * 512:(dc + 1) * 512], in0=f_y[:, :],
                                in1=pv1[:, 0:512], op=ALU.mult).then_inc(G("fxh2"), 2)
            if FST <= 2:
                continue
            for fc in range(32):
                i = 32 * pf + fc
                v.wait_ge(G("fhmm"), i + 1)
                _w(v, G("fhq"), i)
                v.tensor_scalar(out=f_hr[:, :], in0=MM[fc % 2][:, 0:512],
                                scalar1=0.0, scalar2=None, op0=ALU.max).then_inc(G("fhr"), 1)
            if FST <= 3:
                continue
            for dc in range(8):
                i = 8 * pf + dc
                for dcp in ([dc - 1] if dc > 0 and FST > 4 else []):
                    ip = 8 * pf + dcp
                    for tc in range(4):
                        j = 4 * ip + tc
                        v.wait_ge(G("foT"), j + 1)
                        if j >= 4:
                            v.wait_ge(G("fout0" if j % 2 == 0 else "fout1"), 16 * (j // 2 - 1))
                        v.tensor_copy(out=f_os[j % 4][:, :],
                                      in_=MM[j % 2][:, 0:128]).then_inc(G("foev"), 1)
                v.wait_ge(G("fyS"), i + 1)
                pvb = PVB[dc % 2]
                vwait()
                vstep(v.tensor_copy(out=f_y[:, :], in_=pvb[:, 0:512]).then_inc(G("fspk"), 1), "fspk", i + 1)
                vwait()
                vstep(v.tensor_scalar(out=f_spk[:, :], in0=f_y[:, :], scalar1=1.0,
                                      scalar2=None, op0=ALU.is_gt))
                vwait()
                vstep(v.tensor_copy(out=f_acc[:, :], in_=f_spk[:, :]))
                cur = f_y
                bufs = [f_mA, f_mB, f_mC]
                bi = 0
                for st in range(4):
                    tmp = bufs[bi % 3]
                    nxt = bufs[(bi + 1) % 3]
                    bi += 2
                    vwait()
                    vstep(v.scalar_tensor_tensor(out=tmp[:, :], in0=cur[:, :], scalar=0.9,
                                                 in1=f_y[:, :], op0=ALU.mult, op1=ALU.add))
                    vwait()
                    vstep(v.tensor_tensor(out=nxt[:, :], in0=tmp[:, :], in1=f_spk[:, :],
                                          op=ALU.subtract))
                    vwait()
                    vstep(v.tensor_scalar(out=f_spk[:, :], in0=nxt[:, :], scalar1=1.0,
                                          scalar2=None, op0=ALU.is_gt))
                    vwait()
                    vstep(v.tensor_tensor(out=f_acc[:, :], in0=f_acc[:, :], in1=f_spk[:, :],
                                          op=ALU.add))
                    cur = nxt
                if FST > 4:
                    _w(v, G("foT"), 4 * i)
                vwait()
                v.scalar_tensor_tensor(out=f_ofm[:, :], in0=f_acc[:, :], scalar=0.2,
                                       in1=f_x1p[:, dc * 512:(dc + 1) * 512],
                                       op0=ALU.mult, op1=ALU.add).then_inc(G("fof"), 1)
            for dcp in ([7] if FST > 4 else []):
                ip = 8 * pf + dcp
                for tc in range(4):
                    j = 4 * ip + tc
                    v.wait_ge(G("foT"), j + 1)
                    if j >= 4:
                        v.wait_ge(G("fout0" if j % 2 == 0 else "fout1"), 16 * (j // 2 - 1))
                    v.tensor_copy(out=f_os[j % 4][:, :],
                                  in_=MM[j % 2][:, 0:128]).then_inc(G("foev"), 1)

    @block.sync
    def _(sp):
        loads = [
            (ident[:, :], id_in), (eye2[:, :], eye2_in), (blk2[:, :], blk2_in),
            (bq_sb[:, :], bq_in), (bk_sb[:, :], bk_in), (bo_sb[:, :], bo_in),
            (bvrow[:, :], bv_in), (sidx[:, :], sidx_in), (gidx[:, :], gidx_in),
            (fme[:, :], fme_in), (fpr[:, :], fpr_in), (nonce[:, :], non_in),
        ]
        assert len(loads) == NSMALL
        for dst, src in loads:
            sp.dma_start(out=dst, in_=src).then_inc(G("s_small"), 16)

        for i in range(24):
            w_in = [wq_in, wk_in, wv_in][i // 8]
            dc = i % 8
            _w(sp, G("p0wc"), i - 1)
            sp.dma_start(out=p_wst[i % 2][:, :],
                         in_=w_in[dc * 128:(dc + 1) * 128, :]).then_inc(G("p0wld0" if i % 2 == 0 else "p0wld1"), 16)
        def xload(pp):
            _w(sp, G("p0xh"), 4 * (pp - 1))
            sp.dma_start(out=p_xst[pp % 2][:, :].rearrange("p (t d) -> p t d", t=4),
                         in_=x_in[pp * 512:(pp + 1) * 512, :].rearrange("(t p) d -> p t d", p=128)
                         ).then_inc(G("p0x0" if pp % 2 == 0 else "p0x1"), 16)

        xload(0)
        xload(1)
        for pp in range(8):
            for gr in range(12):
                si = 12 * pp + gr
                stage = p_stage[si % 2]
                sp.wait_ge(G("p0st"), si + 1)
                if gr < 8:
                    dstt = qs if gr < 4 else ks
                    fc = gr % 4
                    ins = sp.dma_start(out=dstt.ap()[fc * 128:(fc + 1) * 128,
                                                     pp * 512:(pp + 1) * 512],
                                       in_=stage[:, :])
                else:
                    kcg = 4 * pp + (gr - 8)
                    ins = sp.dma_start(out=vts.ap()[kcg, :, :], in_=stage[:, :])
                ins.then_inc(G("p0sd0" if si % 2 == 0 else "p0sd1"), 16)
            if pp + 2 < 8:
                xload(pp + 2)
        sp.wait_ge(G("p0sd0"), 16 * 48)
        sp.wait_ge(G("p0sd1"), 16 * 48)
        sp.sem_inc(G("bar"), 1)
        sp.wait_ge(G("bar"), 5)
        if upto == "p0":
            return

        def head_load(h):
            hs = "hload0" if h % 2 == 0 else "hload1"
            sp.dma_start(out=a_qh[h % 2][:, :],
                         in_=qs.ap()[h * 64:(h + 1) * 64, :]).then_inc(G(hs), 16)
            sp.dma_start(out=a_kh[h % 2][:, :],
                         in_=ks.ap()[h * 64:(h + 1) * 64, :]).then_inc(G(hs), 16)
            sp.dma_start(out=a_vt[h % 2][:, :].rearrange("p (k f) -> p k f", k=NKC),
                         in_=vts.ap()[:, :, h * 64:(h + 1) * 64].rearrange("k p f -> p k f")
                         ).then_inc(G(hs), 16)

        head_load(0)
        head_load(1)
        for h in range(HL):
            sp.wait_ge(G("evS"), 256 * (h + 1))
            for ppn in range(8):
                e = 8 * h + ppn
                _w(sp, G("ep1"), e)
                _w(sp, G("scl"), e)
                sp.dma_start(out=a_nr[0:1, :],
                             in_=a_hacc[65:66, ppn * 512:(ppn + 1) * 512]).then_inc(G("stg"), 16)
                sp.dma_start(out=a_tq[0:1, :],
                             in_=a_hacc[64:65, ppn * 512:(ppn + 1) * 512]).then_inc(G("stg"), 16)
            if h + 2 < HL:
                sp.wait_ge(G("dot"), 128 * (h + 1))
                sp.wait_ge(G("vtS"), 32 * (h + 1))
                head_load(h + 2)
        if upto == "dump":
            sp.wait_ge(G("attS"), 64)
            sp.dma_start(out=dbg_q, in_=qs.ap()).then_inc(G("fout0"), 16)
            sp.dma_start(out=dbg_k, in_=ks.ap()).then_inc(G("fout0"), 16)
            sp.dma_start(out=dbg_a, in_=attn_all[:, :]).then_inc(G("fout0"), 16)
            sp.dma_start(out=dbg_h, in_=a_hacc[:, :]).then_inc(G("fout0"), 16)
            sp.dma_start(out=dbg_u, in_=a_B1[:, :]).then_inc(G("fout0"), 16)
            sp.dma_start(out=dbg_s, in_=a_s[:, :]).then_inc(G("fout0"), 16)
            sp.dma_start(out=dbg_n, in_=a_ncc[:, :]).then_inc(G("fout0"), 16)
            sp.wait_ge(G("fout0"), 112)
        sp.wait_ge(G("stg"), 32 * 64)
        sp.sem_inc(G("bar"), 1)
        if upto in ("attn", "exch"):
            return
        sp.wait_ge(G("bar"), 10)
        if upto == "gat":
            return
        if upto == "ffn0e":
            sp.dma_start(out=fval[:, :], in_=non_in).then_inc(G("fxld"), 16)
            sp.wait_ge(G("fxld"), 16)
            return
        if upto in ("ffn0f", "ffn0g"):
            sp.dma_start(out=f_x[:, :].rearrange("p (t d) -> p t d", t=4),
                         in_=x_in[0:512, :].rearrange("(t p) d -> p t d", p=128)
                         ).then_inc(G("fxld"), 16)
            sp.wait_ge(G("fxld"), 16)
            return
        if upto == "ffn0d":
            return

        for pf in range(NPF):
            _w(sp, G("fxT"), 32 * pf)
            if upto == "ffn0i":
                sp.sem_inc(G("fxld"), 16)
                continue
            sp.dma_start(out=f_x[:, :].rearrange("p (t d) -> p t d", t=4),
                         in_=x_in[pf * 512:(pf + 1) * 512, :].rearrange("(t p) d -> p t d", p=128)
                         ).then_inc(G("fxld"), 16)
            if FST <= 4:
                continue
            for dc in range(8):
                for tc in range(4):
                    j = 32 * pf + 4 * dc + tc
                    sp.wait_ge(G("foev"), j + 1)
                    sp.dma_start(out=out_o[pf * 512 + tc * 128: pf * 512 + (tc + 1) * 128,
                                           dc * 128:(dc + 1) * 128],
                                 in_=f_os[j % 4][:, :]).then_inc(G("fout0" if j % 2 == 0 else "fout1"), 16)
        if FST > 4:
            sp.wait_ge(G("fout0"), 16 * 16 * NPF)
            sp.wait_ge(G("fout1"), 16 * 16 * NPF)

    @block.gpsimd
    def _(gp):
        gp.sem_inc(G("bar"), 1)
        gp.wait_ge(G("bar"), 5)
        if upto == "p0":
            return
        for i in range(NWT):
            kind, r, c, src = cast_tiles[i]
            if i >= 2:
                gp.wait_ge(G("wcdn"), i - 1)
            gp.dma_start(out=a_wc[i % 2][:, :], in_=src).then_inc(G("wcld0" if i % 2 == 0 else "wcld1"), 16)
            iw = i - 1
            if iw >= NWO:
                kindw, rw, cw, _s = cast_tiles[iw]
                dstt = wfcb if kindw == "wfc" else wpjb
                gp.wait_ge(G("wcdn"), iw + 1)
                oi = iw - NWO
                gp.dma_start(out=dstt.ap()[rw * 128:(rw + 1) * 128, cw * 512:(cw + 1) * 512],
                             in_=a_wb[iw % 2][:, :]).then_inc(G("wcod0" if oi % 2 == 0 else "wcod1"), 16)
        iw = NWT - 1
        kindw, rw, cw, _s = cast_tiles[iw]
        gp.wait_ge(G("wcdn"), iw + 1)
        oi = iw - NWO
        gp.dma_start(out=(wfcb if kindw == "wfc" else wpjb).ap()[rw * 128:(rw + 1) * 128,
                                                                cw * 512:(cw + 1) * 512],
                     in_=a_wb[iw % 2][:, :]).then_inc(G("wcod0" if oi % 2 == 0 else "wcod1"), 16)
        if upto == "attn":
            gp.sem_inc(G("bar"), 1)
            return
        for h in range(HL):
            gp.wait_ge(G("attS"), 8 * (h + 1))
            row0 = 64 * (h % 2)
            cbase = (h // 2) * S
            gp.indirect_dma_start(
                shd.ap(), bass.IndirectOffsetOnAxis(sidx[:, h:h + 1], 0),
                attn_all[row0:row0 + 64, cbase + TM: cbase + S], None,
            ).then_inc(G("exs"), 16)
        gp.wait_ge(G("exs"), 16 * HL)
        gp.wait_ge(G("wcod0"), 16 * ((NWT - NWO) // 2))
        gp.wait_ge(G("wcod1"), 16 * ((NWT - NWO) // 2))
        gp.indirect_dma_start(
            shf.ap(), bass.IndirectOffsetOnAxis(fme[:, 0:1], 0), nonce[:, :], None,
        ).then_inc(G("exs"), 16)
        gp.wait_ge(G("exs"), 16 * (HL + 1))
        with gp.register("cnt") as cnt, gp.register("nr_") as nr_, gp.register("df") as df:
            nv = gp.value_load(nonce[0:1, 0:1])
            gp.reg_mov(nr_, nv)
            gp.reg_mov(cnt, 0)

            def cond():
                gp.reg_add(cnt, cnt, 16)
                gp.indirect_dma_start(
                    fval[:, :], None, shf.ap(),
                    bass.IndirectOffsetOnAxis(fpr[:, 0:1], 0),
                ).then_inc(G("pls"), 16)
                gp.wait_ge(G("pls"), cnt)
                vv = gp.value_load(fval[0:1, 0:1])
                gp.reg_sub(df, vv, nr_)
                return df

            with gp.While(cond):
                gp.nop()
        gp.sem_inc(G("bar"), 1)
        if upto == "exch":
            return
        gp.wait_ge(G("bar"), 10)
        if upto in ("ffn0f", "ffn0g", "ffn0m", "ffn0n", "ffn0p", "ffn0q", "ffn0r", "ffn0u", "ffn0u2", "ffn0u3"):
            return
        if upto == "ffn0d":
            gp.dma_start(out=f_x[:, :].rearrange("p (t d) -> p t d", t=4),
                         in_=x_in[0:512, :].rearrange("(t p) d -> p t d", p=128)
                         ).then_inc(G("fxld"), 16)
            gp.wait_ge(G("fxld"), 16)
            return
        if upto == "ffn0a":
            return
        for pf in range(2 if upto == "gat" else NPF):
            _w(gp, G("fop"), 8 * (pf - 1))
            for fc in range(4):
                gp.indirect_dma_start(
                    f_pap[pf % 2][:, fc * 512:(fc + 1) * 512], None,
                    shd.ap(),
                    bass.IndirectOffsetOnAxis(gidx[:, fc:fc + 1], 0),
                    element_offset=pf * 512,
                ).then_inc(G("gat0" if pf % 2 == 0 else "gat1"), 16)
            if upto == "gat" or FST <= 2:
                continue
            for fc in range(32):
                i = 32 * pf + fc
                _w(gp, G("fhmm"), i - 1)
                gp.dma_start(out=f_w1[i % 2][:, :].rearrange("p (a c) -> p a c", a=8),
                             in_=wfcb.ap()[:, fc * 128:(fc + 1) * 128]
                             .rearrange("(a p) c -> p a c", p=128)).then_inc(G("fw1a" if i % 2 == 0 else "fw1b"), 16)
            if FST <= 3:
                continue
            for dc in range(8):
                i = 8 * pf + dc
                _w(gp, G("fyS"), i - 1)
                gp.dma_start(out=f_w2[i % 2][:, :].rearrange("p (a c) -> p a c", a=32),
                             in_=wpjb.ap()[:, dc * 128:(dc + 1) * 128]
                             .rearrange("(a p) c -> p a c", p=128)).then_inc(G("fw2a" if i % 2 == 0 else "fw2b"), 16)

    return nc, stack


_NC_CACHE = {}


def _get_nc():
    if "nc" not in _NC_CACHE:
        _NC_CACHE["nc"] = build_nc()
    return _NC_CACHE["nc"]


def make_in_maps(L0_attn_x, L0_attn_wq, L0_attn_bq, L0_attn_wk, L0_attn_bk,
                 L0_attn_wv, L0_attn_bv, L0_attn_wo, L0_attn_bo,
                 L0_ffn_W_fc, L0_ffn_W_proj):
    import secrets
    x = np.ascontiguousarray(np.asarray(L0_attn_x, dtype=np.float32))
    wq = np.ascontiguousarray(np.asarray(L0_attn_wq, dtype=np.float32))
    wk = np.ascontiguousarray(np.asarray(L0_attn_wk, dtype=np.float32))
    wv = np.ascontiguousarray(np.asarray(L0_attn_wv, dtype=np.float32))
    bq = np.asarray(L0_attn_bq, dtype=np.float32)
    bk = np.asarray(L0_attn_bk, dtype=np.float32)
    bv = np.asarray(L0_attn_bv, dtype=np.float32)
    wo = np.ascontiguousarray(np.asarray(L0_attn_wo, dtype=np.float32))
    bo = np.asarray(L0_attn_bo, dtype=np.float32)
    wfcT = np.ascontiguousarray(np.asarray(L0_ffn_W_fc, dtype=np.float32).T)
    wpjT = np.ascontiguousarray(np.asarray(L0_ffn_W_proj, dtype=np.float32).T)

    identity = np.eye(128, dtype=np.float32)
    eye2 = np.zeros((128, 2), dtype=np.float32)
    eye2[0:64, 0] = 1.0
    eye2[64:128, 1] = 1.0
    blk2 = np.zeros((2, 128), dtype=np.float32)
    blk2[0, 0:64] = 1.0
    blk2[1, 64:128] = 1.0
    nonce_val = (secrets.randbits(30) | 1)

    in_maps = []
    for cid in range(8):
        b = cid // 2
        hg = cid % 2
        pr = cid ^ 1
        perm = np.concatenate([np.arange(hg * TM, (hg + 1) * TM),
                               np.arange((1 - hg) * TM, (2 - hg) * TM) % S])
        fsl = slice(hg * FQ, (hg + 1) * FQ)
        wo_perm = np.concatenate([wo[hg * FQ:(hg + 1) * FQ, :],
                                  wo[(1 - hg) * FQ:(2 - hg) * FQ, :]], axis=0)
        m = {
            "x_in": np.ascontiguousarray(x[b][perm]),
            "wq_in": np.ascontiguousarray(wq[:, fsl]),
            "wk_in": np.ascontiguousarray(wk[:, fsl]),
            "wv_in": np.ascontiguousarray(wv[:, fsl]),
            "bq_in": np.ascontiguousarray(bq[fsl].reshape(4, 128).T),
            "bk_in": np.ascontiguousarray(bk[fsl].reshape(4, 128).T),
            "bv_in": np.ascontiguousarray(bv[fsl].reshape(1, FQ)),
            "wo_in": np.ascontiguousarray(wo_perm),
            "bo_in": np.ascontiguousarray(bo.reshape(8, 128).T),
            "wfc_in": wfcT,
            "wpj_in": wpjT,
            "id_in": identity,
            "eye2_in": eye2,
            "blk2_in": blk2,
            "sidx_in": np.ascontiguousarray(
                (cid * FQ + np.arange(64)[:, None] + 64 * (np.arange(8)[None, :] % 2)
                 + 128 * (np.arange(8)[None, :] // 2)).astype(np.int32)),
            "gidx_in": np.ascontiguousarray(
                (pr * FQ + np.arange(128)[:, None] + 128 * np.arange(4)[None, :]).astype(np.int32)),
            "fme_in": np.array([[2 * cid], [2 * cid + 1]], dtype=np.int32),
            "fpr_in": np.array([[2 * pr], [2 * pr + 1]], dtype=np.int32),
            "non_in": np.full((2, 16), nonce_val, dtype=np.int32),
        }
        in_maps.append(m)
    return in_maps


def assemble_out(results):
    out = np.empty((4, S, D), dtype=np.float32)
    for cid in range(8):
        b = cid // 2
        hg = cid % 2
        out[b, hg * TM:(hg + 1) * TM, :] = results[cid]["out_o"]
    return out


def kernel(**inputs):
    in_maps = make_in_maps(**inputs)
    nc, _stack = _get_nc()
    res = run_bass_kernel_spmd(nc, in_maps, core_ids=list(range(8)))
    return assemble_out([res.results[c] for c in range(8)])


# version 05
# speedup vs baseline: 1.1024x; 1.1024x over previous
import numpy as np
import concourse.bass as bass
import concourse.mybir as mybir
from concourse.bass_utils import run_bass_kernel_spmd

F32 = mybir.dt.float32
BF16 = mybir.dt.bfloat16
I32 = mybir.dt.int32
AF = mybir.ActivationFunctionType
ALU = mybir.AluOpType

S = 4096
D = 1024
DH = 64
HL = 8
FQ = 512
NKC = 32
NCH = HL * NKC
TM = 2048
FF = 4096
CLIPV = 1.0 - 1e-6
HPI1 = 1.0 + np.pi / 2.0
NWO = 16
NWT = 144
NSMALL = 12


def _w(eng, sem, v):
    if v >= 1:
        eng.wait_ge(sem, v)


def build_nc(upto="full"):
    NPF = 1 if upto.startswith("ffn") else 4
    FSB = {"ffnb1": 1, "ffnb2a": 15, "ffnb2x": 16, "ffnb2y": 17, "ffnb2": 2, "ffnb3": 3}.get(upto, 99)
    FST = {"ffn0": 0, "ffn0a": 0, "ffn0b": 0, "ffn0d": -1, "ffn0e": -1, "ffn0f": -1, "ffn0g": -1,
           "ffn0h": 0, "ffn0i": 0, "ffn0m": -1, "ffn0n": -1, "ffn0p": -1, "ffn0q": -1, "ffn0r": -1, "ffn0u": -1, "ffn0u2": -1, "ffn0u3": -1, "ffna": 1, "ffnb1": 2, "ffnb2": 2, "ffnb2a": 2, "ffnb2x": 2, "ffnb2y": 2, "ffnb3": 2, "ffnb": 2, "ffnc": 3, "ffnd": 4}.get(upto, 9)
    NTT = 1 if upto in ("ffn0h", "ffn0i") else 4
    nc = bass.Bass("TRN2", target_bir_lowering=False, debug=False)

    x_in = nc.dram_tensor("x_in", [S, D], F32, kind="ExternalInput").ap()
    wq_in = nc.dram_tensor("wq_in", [D, FQ], F32, kind="ExternalInput").ap()
    wk_in = nc.dram_tensor("wk_in", [D, FQ], F32, kind="ExternalInput").ap()
    wv_in = nc.dram_tensor("wv_in", [D, FQ], F32, kind="ExternalInput").ap()
    bq_in = nc.dram_tensor("bq_in", [128, 4], F32, kind="ExternalInput").ap()
    bk_in = nc.dram_tensor("bk_in", [128, 4], F32, kind="ExternalInput").ap()
    bv_in = nc.dram_tensor("bv_in", [1, FQ], F32, kind="ExternalInput").ap()
    wo_in = nc.dram_tensor("wo_in", [D, D], F32, kind="ExternalInput").ap()
    bo_in = nc.dram_tensor("bo_in", [128, 8], F32, kind="ExternalInput").ap()
    wfc_in = nc.dram_tensor("wfc_in", [D, FF], F32, kind="ExternalInput").ap()
    wpj_in = nc.dram_tensor("wpj_in", [FF, D], F32, kind="ExternalInput").ap()
    id_in = nc.dram_tensor("id_in", [128, 128], F32, kind="ExternalInput").ap()
    eye2_in = nc.dram_tensor("eye2_in", [128, 2], F32, kind="ExternalInput").ap()
    blk2_in = nc.dram_tensor("blk2_in", [2, 128], F32, kind="ExternalInput").ap()
    sidx_in = nc.dram_tensor("sidx_in", [64, 8], I32, kind="ExternalInput").ap()
    gidx_in = nc.dram_tensor("gidx_in", [128, 4], I32, kind="ExternalInput").ap()
    fme_in = nc.dram_tensor("fme_in", [2, 1], I32, kind="ExternalInput").ap()
    fpr_in = nc.dram_tensor("fpr_in", [2, 1], I32, kind="ExternalInput").ap()
    non_in = nc.dram_tensor("non_in", [2, 16], I32, kind="ExternalInput").ap()
    out_o = nc.dram_tensor("out_o", [TM, D], F32, kind="ExternalOutput").ap()
    dbg_q = nc.dram_tensor("dbg_q", [FQ, S], BF16, kind="ExternalOutput").ap()
    dbg_k = nc.dram_tensor("dbg_k", [FQ, S], BF16, kind="ExternalOutput").ap()
    dbg_a = nc.dram_tensor("dbg_a", [128, 4 * S], BF16, kind="ExternalOutput").ap()
    dbg_h = nc.dram_tensor("dbg_h", [128, S], F32, kind="ExternalOutput").ap()
    dbg_u = nc.dram_tensor("dbg_u", [128, S], F32, kind="ExternalOutput").ap()
    dbg_s = nc.dram_tensor("dbg_s", [128, S], BF16, kind="ExternalOutput").ap()
    dbg_n = nc.dram_tensor("dbg_n", [128, 4], F32, kind="ExternalOutput").ap()

    qs = nc.dram_tensor("qs", [FQ, S], BF16, kind="Internal")
    ks = nc.dram_tensor("ks", [FQ, S], BF16, kind="Internal")
    vts = nc.dram_tensor("vts", [NKC, 128, FQ], BF16, kind="Internal")
    wfcb = nc.dram_tensor("wfcb", [D, FF], BF16, kind="Internal")
    wpjb = nc.dram_tensor("wpjb", [FF, D], BF16, kind="Internal")
    shd = nc.dram_tensor("shd", [8 * FQ, TM], BF16, kind="Internal", addr_space="Shared")
    shf = nc.dram_tensor("shf", [16, 16], I32, kind="Internal", addr_space="Shared")

    A = nc.alloc_sbuf_tensor
    consts = A("consts", [128, 8], F32)
    ident = A("ident", [128, 128], F32)
    eye2 = A("eye2", [128, 2], F32)
    blk2 = A("blk2", [2, 128], F32)
    onesrow = A("onesrow", [1, 128], F32)
    onescol = A("onescol", [128, 1], F32)
    ones2 = A("ones2", [128, 2], F32)
    bq_sb = A("bq_sb", [128, 4], F32)
    bk_sb = A("bk_sb", [128, 4], F32)
    bo_sb = A("bo_sb", [128, 8], F32)
    bvrow = A("bvrow", [1, FQ], F32)
    bvbc = A("bvbc", [128, FQ], F32)
    sidx = A("sidx", [64, 8], I32)
    gidx = A("gidx", [128, 4], I32)
    fme = A("fme", [2, 1], I32)
    fpr = A("fpr", [2, 1], I32)
    nonce = A("nonce", [2, 16], I32)
    fval = A("fval", [2, 16], I32)
    wo_sb = A("wo_sb", [128, 8 * D], BF16)

    used = nc.SBUF_PARTITION_SIZE_BYTES - nc.sbuf_bytes_remaining
    AB = ((used + 255) // 256) * 256 + 256
    CAP = 229344

    def ar(name, shape, dtype, off):
        nbytes = shape[-1] * mybir.dt.size(dtype)
        assert AB + off + nbytes <= CAP, (name, AB, off, nbytes)
        return nc.alloc_sbuf_tensor_at(name, shape, dtype, offset=AB + off)

    attn_all = ar("attn_all", [128, 4 * S], BF16, 0)

    p_xst = [ar("p_xst0", [128, 4096], F32, 0), ar("p_xst1", [128, 4096], F32, 16384)]
    p_xhat = ar("p_xhat", [128, 4096], F32, 32768)
    p_xhT = ar("p_xhT", [128, 4096], BF16, 49152)
    p_wqsb = ar("p_wqsb", [128, 4096], BF16, 57344)
    p_wksb = ar("p_wksb", [128, 4096], BF16, 65536)
    p_wvsb = ar("p_wvsb", [128, 4096], BF16, 73728)
    p_wst = [ar("p_wst0", [128, 512], F32, 81920), ar("p_wst1", [128, 512], F32, 83968)]
    p_ybuf = ar("p_ybuf", [128, 512], F32, 86016)
    p_ysq = ar("p_ysq", [128, 512], F32, 88064)
    p_stage = [ar("p_stage0", [128, 512], BF16, 90112), ar("p_stage1", [128, 512], BF16, 91136)]
    p_stat = ar("p_stat", [128, 8], F32, 92160)
    p_nrm = ar("p_nrm", [2, 512], F32, 92416)
    p_inv = ar("p_inv", [2, 512], F32, 94464)
    p_sqd = ar("p_sqd", [128, 1024], F32, 96512)

    a_qh = [ar("a_qh0", [64, S], BF16, 32768), ar("a_qh1", [64, S], BF16, 40960)]
    a_kh = [ar("a_kh0", [64, S], BF16, 49152), ar("a_kh1", [64, S], BF16, 57344)]
    a_vt = [ar("a_vt0", [128, NKC * DH], BF16, 65536), ar("a_vt1", [128, NKC * DH], BF16, 69632)]
    a_B0 = [ar("a_B00", [128, S], F32, 73728), ar("a_B01", [128, S], F32, 90112)]
    a_B1 = ar("a_B1", [128, S], F32, 106496)
    a_B2 = ar("a_B2", [128, S], F32, 122880)
    a_s = ar("a_s", [128, S], BF16, 139264)
    a_hacc = ar("a_hacc", [128, S], F32, 147456)
    a_vtt = [ar("a_vtt0", [128, 66], BF16, 163840), ar("a_vtt1", [128, 66], BF16, 164096)]
    a_ncc = ar("a_ncc", [128, 4], F32, 164352)
    a_nr = ar("a_nr", [1, 512], F32, 164608)
    a_tq = ar("a_tq", [1, 512], F32, 166656)
    a_sa = ar("a_sa", [1, 512], F32, 168704)
    a_sb = ar("a_sb", [1, 512], F32, 170752)
    a_sc = ar("a_sc", [1, 512], F32, 172800)
    a_sd = ar("a_sd", [1, 512], F32, 174848)
    a_wc = [ar("a_wc0", [128, 512], F32, 176896), ar("a_wc1", [128, 512], F32, 178944)]
    a_wb = [ar("a_wb0", [128, 512], BF16, 180992), ar("a_wb1", [128, 512], BF16, 182016)]

    f_x = ar("f_x", [128, 4096], F32, 32768)
    f_xfm = ar("f_xfm", [128, 4096], F32, 49152)
    f_x1p = ar("f_x1p", [128, 4096], F32, 65536)
    f_xh2 = ar("f_xh2", [128, 4096], BF16, 81920)
    f_xsq = ar("f_xsq", [128, 512], F32, 90112)
    f_h = ar("f_h", [128, 16384], BF16, 92160)
    f_w1 = [ar("f_w10", [128, 1024], BF16, 124928), ar("f_w11", [128, 1024], BF16, 126976)]
    f_w2 = [ar("f_w20", [128, 4096], BF16, 129024), ar("f_w21", [128, 4096], BF16, 137216)]
    f_pap = [ar("f_pap0", [128, 2048], BF16, 145408), ar("f_pap1", [128, 2048], BF16, 149504)]
    f_y = ar("f_y", [128, 512], F32, 153600)
    f_mA = ar("f_mA", [128, 512], F32, 155648)
    f_mB = ar("f_mB", [128, 512], F32, 157696)
    f_mC = ar("f_mC", [128, 512], F32, 159744)
    f_spk = ar("f_spk", [128, 512], F32, 161792)
    f_acc = ar("f_acc", [128, 512], F32, 163840)
    f_hr = ar("f_hr", [128, 512], F32, 165888)
    f_ofm = ar("f_ofm", [128, 512], F32, 167936)
    f_os = [ar("f_os0", [128, 128], F32, 169984), ar("f_os1", [128, 128], F32, 170496),
            ar("f_os2", [128, 128], F32, 171008), ar("f_os3", [128, 128], F32, 171520)]
    f_st = ar("f_st", [2, 512], F32, 178176)
    f_st2 = ar("f_st2", [2, 512], F32, 180224)
    f_st3 = ar("f_st3", [1, 512], F32, 182272)
    f_st4 = ar("f_st4", [1, 512], F32, 184320)

    psA = nc.alloc_psum_tensor("psA", [128, 1024], F32)
    psB = nc.alloc_psum_tensor("psB", [128, 1024], F32)
    pv0 = nc.alloc_psum_tensor("pv0", [128, 512], F32)
    pv1 = nc.alloc_psum_tensor("pv1", [128, 512], F32)
    mm0 = nc.alloc_psum_tensor("mm0", [128, 512], F32)
    mm1 = nc.alloc_psum_tensor("mm1", [128, 512], F32)
    PVB = [pv0, pv1]
    MM = [mm0, mm1]

    semnames = [
        "s_small", "s_dve", "s_bv",
        "p0wld0", "p0wld1", "p0wc", "p0x0", "p0x1", "p0sq", "p0var", "p0rstd", "p0xh",
        "p0T", "p0Tev", "p0mm", "p0yb", "p0ysq", "p0nrmm", "p0sqr", "p0inv",
        "p0bc", "p0fin", "p0st", "p0sd0", "p0sd1",
        "bar",
        "hload0", "hload1", "dot", "clip", "lnS", "dS", "sS", "cS", "vtS", "pvS", "evS",
        "stg", "ep1", "scl", "bcS", "attS",
        "wcld0", "wcld1", "wcdn", "wcod0", "wcod1", "wcst",
        "exs", "pls", "gat0", "gat1",
        "fxld", "fxT", "fxfm", "fop", "fx1", "fstq", "fsq2", "fstm", "fvar",
        "frst", "fstc", "fbc", "fxh2", "fw1a", "fw1b", "fhmm", "fhr", "fhq", "fw2a", "fw2b", "fyS",
        "fspk", "fof", "foT", "foev", "fout0", "fout1", "vch", "sch",
    ]
    from contextlib import ExitStack
    stack = ExitStack()
    sems = {n: stack.enter_context(nc.semaphore(n)) for n in semnames}
    G = sems.__getitem__

    cast_tiles = []
    for i in range(NWO):
        r, c = i // 2, i % 2
        cast_tiles.append(("wo", r, c, wo_in[r * 128:(r + 1) * 128, c * 512:(c + 1) * 512]))
    for j in range(64):
        r, c = j // 8, j % 8
        cast_tiles.append(("wfc", r, c, wfc_in[r * 128:(r + 1) * 128, c * 512:(c + 1) * 512]))
    for j in range(64):
        r, c = j // 2, j % 2
        cast_tiles.append(("wpj", r, c, wpj_in[r * 128:(r + 1) * 128, c * 512:(c + 1) * 512]))

    block = stack.enter_context(nc.Block())

    @block.tensor
    def _(t):
        t.wait_ge(G("s_small"), 16 * NSMALL)
        t.wait_ge(G("s_dve"), 1)
        t.matmul(mm0[:, 0:512], onesrow[0:1, :], bvrow[0:1, :], start=True, stop=True).then_inc(G("p0bc"), 1)

        for pp in range(8):
            _w(t, G("p0st"), 12 * pp)
            for tt in range(4):
                gi = 4 * pp + tt
                _w(t, G("p0xh"), gi + 1)
                for j in range(8):
                    i = 8 * gi + j
                    _w(t, G("p0Tev"), i - 1)
                    t.transpose(MM[i % 2][:, 0:128],
                                p_xhat[:, tt * 1024 + j * 128: tt * 1024 + (j + 1) * 128],
                                ident[:, :]).then_inc(G("p0T"), 1)
            if pp == 0:
                t.wait_ge(G("p0wc"), 24)
            _w(t, G("p0Tev"), 32 * (pp + 1))
            for gr in range(12):
                gg = 12 * pp + gr
                pv = PVB[gr % 2]
                gp2 = gg - 2
                if gp2 >= 0:
                    p2, g2 = divmod(gp2, 12)
                    if g2 < 8:
                        _w(t, G("p0yb"), 8 * p2 + g2 + 1)
                    else:
                        _w(t, G("p0st"), gp2 + 1)
                if gr < 8:
                    w_sb = p_wqsb if gr < 4 else p_wksb
                    fc = gr % 4
                    for dc in range(8):
                        ins = t.matmul(pv[:, 0:512],
                                       w_sb[:, dc * 512 + fc * 128: dc * 512 + (fc + 1) * 128],
                                       p_xhT[:, dc * 512:(dc + 1) * 512],
                                       start=(dc == 0), stop=(dc == 7))
                else:
                    tc = gr - 8
                    for dc in range(8):
                        ins = t.matmul(pv[:, 0:512],
                                       p_xhT[:, dc * 512 + tc * 128: dc * 512 + (tc + 1) * 128],
                                       p_wvsb[:, dc * 512:(dc + 1) * 512],
                                       start=(dc == 0), stop=(dc == 7))
                ins.then_inc(G("p0mm"), 1)
                for grn in ([gr - 1] if 1 <= gr <= 8 else []):
                    gg2 = 8 * pp + grn
                    mm = MM[grn % 2]
                    _w(t, G("p0ysq"), gg2 + 1)
                    _w(t, G("p0st"), 12 * pp + grn - 1)
                    t.matmul(mm[0:2, 0:512], eye2[:, :], p_ysq[:, 0:512],
                             start=True, stop=True).then_inc(G("p0nrmm"), 1)
                    _w(t, G("p0inv"), gg2 + 1)
                    t.matmul(mm[:, 0:512], blk2[:, :], p_inv[0:2, 0:512],
                             start=True, stop=True).then_inc(G("p0bc"), 1)

        t.wait_ge(G("p0bc"), 65)
        t.sem_inc(G("bar"), 1)
        t.wait_ge(G("bar"), 5)
        if upto == "p0":
            return

        def emit_pv(kk):
            _w(t, G("vtS"), kk + 1)
            for p in range(8):
                pvb = PVB[(8 * kk + p) % 2]
                _w(t, G("evS"), 8 * kk + p - 1)
                t.matmul(pvb[0:66, 0:512], a_vtt[kk % 2][:, :],
                         a_s[:, p * 512:(p + 1) * 512],
                         start=True, stop=True).then_inc(G("pvS"), 1)

        def emit_epi_bc(h):
            for ppn in range(8):
                e = 8 * h + ppn
                _w(t, G("scl"), e + 1)
                _w(t, G("attS"), e - 1)
                t.matmul(MM[ppn % 2][0:64, 0:512], onesrow[0:1, 0:64], a_sd[0:1, :],
                         start=True, stop=True).then_inc(G("bcS"), 1)

        for kk in range(NCH):
            h, kc = divmod(kk, NKC)
            if kc == 0:
                t.wait_ge(G("hload0" if h % 2 == 0 else "hload1"), 48 * (h // 2 + 1))
            for q in range(4):
                Q = 4 * kk + q
                _w(t, G("clip"), Q - 1)
                pb = psA if q % 2 == 0 else psB
                t.matmul(pb[:, 0:512], a_kh[h % 2][0:64, kc * 128:(kc + 1) * 128],
                         a_qh[h % 2][0:64, q * 1024: q * 1024 + 512],
                         start=True, stop=True)
                t.matmul(pb[:, 512:1024], a_kh[h % 2][0:64, kc * 128:(kc + 1) * 128],
                         a_qh[h % 2][0:64, q * 1024 + 512: (q + 1) * 1024],
                         start=True, stop=True).then_inc(G("dot"), 1)
            if kk > 0:
                emit_pv(kk - 1)
            if kc == 0 and kk > 0:
                emit_epi_bc(h - 1)
        emit_pv(NCH - 1)
        emit_epi_bc(HL - 1)

        t.wait_ge(G("bcS"), 64)
        t.wait_ge(G("pvS"), 8 * NCH)
        t.sem_inc(G("bar"), 1)
        if upto in ("attn", "exch"):
            return
        t.wait_ge(G("bar"), 10)
        if upto == "ffn0m":
            t.wait_ge(G("wcst"), 1)
            return
        if upto in ("ffn0p", "ffn0q", "ffn0r", "ffn0u", "ffn0u2", "ffn0u3"):
            nT = {"ffn0p": 1, "ffn0q": 2}.get(upto, 8)
            HS = upto.startswith("ffn0u")
            t.wait_ge(G("wcst"), 1)
            t.wait_ge(G("fxld"), 16)
            for i in range(nT):
                if HS:
                    _w(t, G("fxfm"), i - 1)
                t.transpose(MM[i % 2][:, 0:128], f_x[:, i * 128:(i + 1) * 128],
                            ident[:, :]).then_inc(G("fxT"), 1)
            return
        if upto == "ffn0n":
            t.wait_ge(G("wcdn"), NWT)
            return
        if upto == "ffn0g":
            t.wait_ge(G("fxld"), 16)
            t.transpose(MM[0][:, 0:128], f_x[:, 0:128], ident[:, :])
            return
        if upto in ("gat", "ffn0d", "ffn0e", "ffn0f"):
            return

        t.wait_ge(G("wcst"), 1)

        def emit_oT(i):
            for tc in range(4):
                j = 4 * i + tc
                _w(t, G("foev"), j - 1)
                t.transpose(MM[j % 2][:, 0:128], f_ofm[:, tc * 128:(tc + 1) * 128],
                            ident[:, :]).then_inc(G("foT"), 1)

        for pf in range(NPF):
            t.wait_ge(G("fxld"), 16 * (pf + 1))
            for tt in range(NTT):
                for j in range(8):
                    i = 32 * pf + 8 * tt + j
                    _w(t, G("fxfm"), i - 1)
                    t.transpose(MM[i % 2][:, 0:128],
                                f_x[:, tt * 1024 + j * 128: tt * 1024 + (j + 1) * 128],
                                ident[:, :]).then_inc(G("fxT"), 1)
            if FST <= 0:
                continue
            t.wait_ge(G("gat0" if pf % 2 == 0 else "gat1"), 64 * (pf // 2 + 1))
            t.wait_ge(G("fxfm"), 32 * (pf + 1))
            for dc in range(8):
                i = 8 * pf + dc
                _w(t, G("fx1"), i - 1)
                for fc in range(8):
                    if fc < 4:
                        rhs = attn_all[:, fc * S + pf * 512: fc * S + pf * 512 + 512]
                    else:
                        rhs = f_pap[pf % 2][:, (fc - 4) * 512:(fc - 3) * 512]
                    ins = t.matmul(MM[dc % 2][:, 0:512],
                                   wo_sb[:, fc * D + dc * 128: fc * D + (dc + 1) * 128],
                                   rhs, start=(fc == 0), stop=(fc == 7))
                ins.then_inc(G("fop"), 1)
            if FST <= 1:
                continue
            for dc in range(8):
                i = 8 * pf + dc
                t.wait_ge(G("fstq"), i + 1)
                t.matmul(pv1[0:2, 0:512], ones2[:, :], f_xsq[:, 0:512],
                         start=(dc == 0), stop=(dc == 7)).then_inc(G("fsq2"), 1)
                t.wait_ge(G("fx1"), i + 1)
                ins = t.matmul(pv0[0:2, 0:512], ones2[:, :], f_x1p[:, dc * 512:(dc + 1) * 512],
                               start=(dc == 0), stop=(dc == 7))
            ins.then_inc(G("fstm"), 1)
            if FSB <= 2:
                continue
            t.wait_ge(G("frst"), pf + 1)
            t.matmul(pv0[:, 0:512], onesrow[0:1, :], f_st[0:1, :], start=True, stop=True)
            t.matmul(pv1[:, 0:512], onesrow[0:1, :], f_st3[0:1, :],
                     start=True, stop=True).then_inc(G("fbc"), 1)
            if FST <= 2:
                continue
            t.wait_ge(G("fxh2"), 16 * (pf + 1))
            for fc in range(32):
                i = 32 * pf + fc
                t.wait_ge(G("fw1a" if i % 2 == 0 else "fw1b"), 16 * (i // 2 + 1))
                _w(t, G("fhr"), i - 1)
                for dc in range(8):
                    ins = t.matmul(MM[fc % 2][:, 0:512],
                                   f_w1[i % 2][:, dc * 128:(dc + 1) * 128],
                                   f_xh2[:, dc * 512:(dc + 1) * 512],
                                   start=(dc == 0), stop=(dc == 7))
                ins.then_inc(G("fhmm"), 1)
            if FST <= 3:
                continue
            t.wait_ge(G("fhq"), 32 * (pf + 1))
            for dc in range(8):
                i = 8 * pf + dc
                t.wait_ge(G("fw2a" if i % 2 == 0 else "fw2b"), 16 * (i // 2 + 1))
                _w(t, G("fspk"), i - 1)
                for fc in range(32):
                    ins = t.matmul(PVB[dc % 2][:, 0:512],
                                   f_w2[i % 2][:, fc * 128:(fc + 1) * 128],
                                   f_h[:, fc * 512:(fc + 1) * 512],
                                   start=(fc == 0), stop=(fc == 31))
                ins.then_inc(G("fyS"), 1)
                if FST <= 4:
                    continue
                if dc > 0:
                    _w(t, G("fof"), i)
                    emit_oT(i - 1)
            if FST > 4:
                _w(t, G("fof"), 8 * pf + 8)
                emit_oT(8 * pf + 7)

    @block.scalar
    def _(sc):
        scn = [0]
        slast = [None]

        def sstep(ins, sem=None, val=None):
            if sem is None:
                ins.then_inc(G("sch"), 1)
                scn[0] += 1
                slast[0] = ("sch", scn[0])
            else:
                slast[0] = (sem, val)

        def swait():
            if slast[0] is not None:
                _w(sc, G(slast[0][0]), slast[0][1])

        c_tanh = consts[:, 0:1]
        c_hpi = consts[:, 1:2]
        c_m65 = consts[:, 2:3]
        c_mh = consts[:, 3:4]
        c_eps = consts[:, 4:5]
        c_p80 = consts[:, 5:6]
        c_b40 = consts[:, 6:7]

        sc.wait_ge(G("s_dve"), 1)
        sc.wait_ge(G("s_small"), 16 * NSMALL)

        for pp in range(8):
            xst = p_xst[pp % 2]
            for tt in range(4):
                gi = 4 * pp + tt
                sc.wait_ge(G("p0x0" if pp % 2 == 0 else "p0x1"), 16 * (pp // 2 + 1))
                _w(sc, G("p0var"), gi)
                sc.activation(out=p_sqd[:, :], in_=xst[:, tt * 1024:(tt + 1) * 1024],
                              func=AF.Square, accum_out=p_stat[:, 1:2]).then_inc(G("p0sq"), 1)
                _w(sc, G("p0var"), gi + 1)
                _w(sc, G("p0xh"), gi)
                swait()
                sstep(sc.activation(out=p_stat[:, 5:6], in_=p_stat[:, 6:7], func=AF.Ln,
                                    bias=c_eps, scale=1.0))
                swait()
                sstep(sc.activation(out=p_stat[:, 7:8], in_=p_stat[:, 5:6], func=AF.Exp,
                              scale=c_mh).then_inc(G("p0rstd"), 1), "p0rstd", gi + 1)
            for gr in range(8):
                gg = 8 * pp + gr
                _w(sc, G("p0yb"), 8 * pp + gr + 1)
                sc.activation(out=p_ysq[:, :], in_=p_ybuf[:, :],
                              func=AF.Square).then_inc(G("p0ysq"), 1)
                _w(sc, G("p0nrmm"), gg + 1)
                sc.activation(out=p_nrm[0:2, :], in_=MM[gr % 2][0:2, 0:512],
                              func=AF.Sqrt).then_inc(G("p0sqr"), 1)

        sc.wait_ge(G("p0sqr"), 64)
        sc.sem_inc(G("bar"), 1)
        sc.wait_ge(G("bar"), 5)
        if upto == "p0":
            return

        for kk in range(NCH):
            h, kc = divmod(kk, NKC)
            B0 = a_B0[kk % 2]
            H0 = slice(0, 2048)
            H1 = slice(2048, 4096)
            sc.wait_ge(G("clip"), 4 * kk + 2)
            _w(sc, G("dS"), 2 * kk)
            swait()
            sstep(sc.activation(out=a_B1[:, H0], in_=B0[:, H0], func=AF.Ln, scale=-1.0, bias=1.0))
            sstep(sc.activation(out=a_B2[:, H0], in_=B0[:, H0], func=AF.Ln, scale=1.0,
                                bias=1.0).then_inc(G("lnS"), 1), "lnS", 2 * kk + 1)
            sc.wait_ge(G("clip"), 4 * kk + 4)
            sstep(sc.activation(out=a_B1[:, H1], in_=B0[:, H1], func=AF.Ln, scale=-1.0, bias=1.0))
            sstep(sc.activation(out=a_B2[:, H1], in_=B0[:, H1], func=AF.Ln, scale=1.0,
                                bias=1.0).then_inc(G("lnS"), 1), "lnS", 2 * kk + 2)
            sc.wait_ge(G("dS"), 2 * kk + 1)
            sstep(sc.activation(out=a_B1[:, H0], in_=B0[:, H0], func=AF.Tanh, scale=c_tanh))
            swait()
            sstep(sc.activation(out=a_B2[:, H0], in_=a_B1[:, H0], func=AF.Arctan))
            swait()
            sstep(sc.activation(out=a_B1[:, H0], in_=a_B2[:, H0], func=AF.Ln, scale=-2.0, bias=c_hpi))
            sc.wait_ge(G("dS"), 2 * kk + 2)
            swait()
            sstep(sc.activation(out=a_B1[:, H1], in_=B0[:, H1], func=AF.Tanh, scale=c_tanh))
            swait()
            sstep(sc.activation(out=a_B2[:, H1], in_=a_B1[:, H1], func=AF.Arctan))
            swait()
            sstep(sc.activation(out=a_B1[:, H1], in_=a_B2[:, H1], func=AF.Ln, scale=-2.0, bias=c_hpi))
            _w(sc, G("pvS"), 8 * kk)
            swait()
            sstep(sc.activation(out=a_s[:, :], in_=a_B1[:, :], func=AF.Exp, scale=c_m65,
                                accum_out=a_ncc[:, 0:1]).then_inc(G("sS"), 1), "sS", kk + 1)
            _w(sc, G("vtS"), kk)
            swait()
            sstep(sc.activation(out=a_ncc[:, 1:2], in_=a_ncc[:, 0:1], func=AF.Ln,
                                scale=c_p80))
            swait()
            sstep(sc.activation(out=a_ncc[:, 2:3], in_=a_ncc[:, 1:2], func=AF.Exp,
                          scale=c_mh, bias=c_b40).then_inc(G("cS"), 1), "cS", kk + 1)
            if kc == NKC - 1:
                for ppn in range(8):
                    e = 8 * h + ppn
                    sc.wait_ge(G("stg"), 32 * (e + 1))
                    swait()
                    sstep(sc.activation(out=a_sa[0:1, :], in_=a_nr[0:1, :], func=AF.Ln,
                                        scale=c_p80[0:1, :]))
                    swait()
                    sstep(sc.activation(out=a_sb[0:1, :], in_=a_sa[0:1, :], func=AF.Exp,
                                  scale=c_mh[0:1, :], bias=c_b40[0:1, :]).then_inc(G("ep1"), 1), "ep1", e + 1)

        sc.wait_ge(G("ep1"), 64)
        sc.wait_ge(G("cS"), NCH)
        sc.sem_inc(G("bar"), 1)
        if upto in ("attn", "exch"):
            return
        sc.wait_ge(G("bar"), 10)
        if upto in ("gat", "ffn0d", "ffn0e", "ffn0f", "ffn0g", "ffn0m", "ffn0n", "ffn0p", "ffn0q", "ffn0r", "ffn0u", "ffn0u2", "ffn0u3"):
            return

        if FST <= 1:
            return
        for pf in range(NPF):
            for dc in range(8):
                i = 8 * pf + dc
                sc.wait_ge(G("fx1"), i + 1)
                _w(sc, G("fsq2"), i)
                sc.activation(out=f_xsq[:, :], in_=f_x1p[:, dc * 512:(dc + 1) * 512],
                              func=AF.Square).then_inc(G("fstq"), 1)
            if FSB <= 1 or FSB in (15, 16, 17):
                continue
            sc.wait_ge(G("fvar"), pf + 1)
            sstep(sc.activation(out=f_st2[0:1, :], in_=f_st4[0:1, :], func=AF.Ln,
                                bias=c_eps[0:1, :], scale=1.0))
            swait()
            sc.activation(out=f_st3[0:1, :], in_=f_st2[0:1, :], func=AF.Exp,
                          scale=c_mh[0:1, :]).then_inc(G("frst"), 1)
            if FST <= 2:
                continue
            for fc in range(32):
                i = 32 * pf + fc
                sc.wait_ge(G("fhr"), i + 1)
                if pf > 0 and fc == 0:
                    sc.wait_ge(G("fyS"), 8 * pf)
                sc.activation(out=f_h[:, fc * 512:(fc + 1) * 512], in_=f_hr[:, :],
                              func=AF.Square).then_inc(G("fhq"), 1)

    @block.vector
    def _(v):
        vcn = [0]
        vlast = [None]

        def vstep(ins, sem=None, val=None):
            if sem is None:
                ins.then_inc(G("vch"), 1)
                vcn[0] += 1
                vlast[0] = ("vch", vcn[0])
            else:
                vlast[0] = (sem, val)

        def vwait():
            if vlast[0] is not None:
                _w(v, G(vlast[0][0]), vlast[0][1])

        v.memset(consts[:, 0:1], 0.25)
        v.memset(consts[:, 1:2], HPI1)
        v.memset(consts[:, 2:3], -65.0)
        v.memset(consts[:, 3:4], -0.5)
        v.memset(consts[:, 4:5], 1e-5)
        v.memset(consts[:, 5:6], 1.2089258196146292e24)
        v.memset(consts[:, 6:7], 27.725887222397812)
        v.memset(onesrow[:, :], 1.0)
        v.memset(onescol[:, :], 1.0)
        v.memset(ones2[:, :], 1.0)
        v.memset(a_vtt[0][:, 65:66], 1.0)
        v.memset(a_vtt[1][:, 65:66], 1.0).then_inc(G("s_dve"), 1)
        v.wait_ge(G("p0bc"), 1)
        v.tensor_copy(out=bvbc[:, :], in_=mm0[:, 0:512]).then_inc(G("s_bv"), 1)

        for i in range(24):
            wsb = [p_wqsb, p_wksb, p_wvsb][i // 8]
            dc = i % 8
            v.wait_ge(G("p0wld0" if i % 2 == 0 else "p0wld1"), 16 * (i // 2 + 1))
            v.tensor_copy(out=wsb[:, dc * 512:(dc + 1) * 512],
                          in_=p_wst[i % 2][:, :]).then_inc(G("p0wc"), 1)
        for pp in range(8):
            xst = p_xst[pp % 2]
            _w(v, G("p0mm"), 12 * pp)
            for tt in range(4):
                gi = 4 * pp + tt
                v.wait_ge(G("p0x0" if pp % 2 == 0 else "p0x1"), 16 * (pp // 2 + 1))
                vwait()
                vstep(v.reduce_sum(out=p_stat[:, 0:1], in_=xst[:, tt * 1024:(tt + 1) * 1024],
                                   axis=mybir.AxisListType.X))
                v.wait_ge(G("p0sq"), gi + 1)
                vwait()
                vstep(v.tensor_scalar(out=p_stat[:, 0:1], in0=p_stat[:, 0:1],
                                      scalar1=1.0 / D, scalar2=None, op0=ALU.mult))
                vstep(v.tensor_scalar(out=p_stat[:, 1:2], in0=p_stat[:, 1:2],
                                      scalar1=1.0 / D, scalar2=None, op0=ALU.mult))
                vwait()
                vstep(v.tensor_tensor(out=p_stat[:, 2:3], in0=p_stat[:, 0:1],
                                      in1=p_stat[:, 0:1], op=ALU.mult))
                vwait()
                v.tensor_tensor(out=p_stat[:, 6:7], in0=p_stat[:, 1:2],
                                in1=p_stat[:, 2:3], op=ALU.subtract).then_inc(G("p0var"), 1)
                v.wait_ge(G("p0rstd"), gi + 1)
                if tt == 0:
                    _w(v, G("p0T"), 32 * pp)
                vstep(v.tensor_scalar(out=p_xhat[:, tt * 1024:(tt + 1) * 1024],
                                      in0=xst[:, tt * 1024:(tt + 1) * 1024],
                                      scalar1=p_stat[:, 0:1], scalar2=p_stat[:, 7:8],
                                      op0=ALU.subtract, op1=ALU.mult).then_inc(G("p0xh"), 1), "p0xh", gi + 1)
            for i8 in range(32):
                i = 32 * pp + i8
                tt = i8 // 8
                j = i8 % 8
                v.wait_ge(G("p0T"), i + 1)
                v.tensor_copy(out=p_xhT[:, j * 512 + tt * 128: j * 512 + (tt + 1) * 128],
                              in_=MM[i % 2][:, 0:128]).then_inc(G("p0Tev"), 1)
            for gr in range(12):
                gg = 12 * pp + gr
                si = gg
                stage = p_stage[si % 2]
                v.wait_ge(G("p0mm"), gg + 1)
                if gr < 8:
                    e = 8 * pp + gr
                    bias = (bq_sb if gr < 4 else bk_sb)[:, (gr % 4):(gr % 4) + 1]
                    vwait()
                    vstep(v.tensor_scalar(out=p_ybuf[:, :], in0=PVB[gr % 2][:, 0:512],
                                          scalar1=bias, scalar2=None,
                                          op0=ALU.add).then_inc(G("p0yb"), 1), "p0yb", 8 * pp + gr + 1)
                    v.wait_ge(G("p0sqr"), e + 1)
                    vstep(v.tensor_scalar(out=p_nrm[0:2, :], in0=p_nrm[0:2, :],
                                          scalar1=1e-12, scalar2=None, op0=ALU.max))
                    vwait()
                    v.reciprocal(out=p_inv[0:2, :], in_=p_nrm[0:2, :]).then_inc(G("p0inv"), 1)
                    v.wait_ge(G("p0bc"), e + 2)
                    if si >= 2:
                        v.wait_ge(G("p0sd0" if si % 2 == 0 else "p0sd1"), 16 * (si // 2))
                    vwait()
                    vstep(v.tensor_tensor(out=stage[:, :], in0=p_ybuf[:, :],
                                          in1=MM[gr % 2][:, 0:512],
                                          op=ALU.mult).then_inc(G("p0st"), 1), "p0st", si + 1)
                else:
                    v.wait_ge(G("s_bv"), 1)
                    if si >= 2:
                        v.wait_ge(G("p0sd0" if si % 2 == 0 else "p0sd1"), 16 * (si // 2))
                    vstep(v.tensor_tensor(out=stage[:, :], in0=PVB[gr % 2][:, 0:512],
                                          in1=bvbc[:, :],
                                          op=ALU.add).then_inc(G("p0st"), 1), "p0st", si + 1)

        v.wait_ge(G("p0st"), 96)
        v.sem_inc(G("bar"), 1)
        v.wait_ge(G("bar"), 5)
        if upto == "p0":
            return

        def emit_clip(kk):
            B0 = a_B0[kk % 2]
            _w(v, G("sS"), kk - 1)
            for q in range(4):
                Q = 4 * kk + q
                v.wait_ge(G("dot"), Q + 1)
                pb = psA if q % 2 == 0 else psB
                v.tensor_scalar(out=B0[:, q * 1024:(q + 1) * 1024], in0=pb[:, :],
                                scalar1=CLIPV, scalar2=-CLIPV,
                                op0=ALU.min, op1=ALU.max).then_inc(G("clip"), 1)

        emit_clip(0)
        for kk in range(NCH):
            h, kc = divmod(kk, NKC)
            B0 = a_B0[kk % 2]
            v.wait_ge(G("lnS"), 2 * kk + 1)
            v.tensor_tensor(out=B0[:, 0:2048], in0=a_B2[:, 0:2048], in1=a_B1[:, 0:2048],
                            op=ALU.subtract).then_inc(G("dS"), 1)
            v.wait_ge(G("lnS"), 2 * kk + 2)
            v.tensor_tensor(out=B0[:, 2048:4096], in0=a_B2[:, 2048:4096], in1=a_B1[:, 2048:4096],
                            op=ALU.subtract).then_inc(G("dS"), 1)
            if kk + 1 < NCH:
                emit_clip(kk + 1)
            if kk < NWT:
                i = kk
                v.wait_ge(G("wcld0" if i % 2 == 0 else "wcld1"), 16 * (i // 2 + 1))
                if i < NWO:
                    kind, r, c, _src = cast_tiles[i]
                    dst = wo_sb[:, r * 1024 + c * 512: r * 1024 + (c + 1) * 512]
                else:
                    io = i - NWO
                    if io >= 2:
                        v.wait_ge(G("wcod0" if io % 2 == 0 else "wcod1"), 16 * (io // 2))
                    dst = a_wb[i % 2][:, :]
                v.tensor_copy(out=dst, in_=a_wc[i % 2][:, :]).then_inc(G("wcdn"), 1)
                if i == NWO - 1:
                    v.sem_inc(G("wcst"), 1)
            v.wait_ge(G("cS"), kk + 1)
            v.wait_ge(G("hload0" if h % 2 == 0 else "hload1"), 48 * (h // 2 + 1))
            _w(v, G("pvS"), 8 * (kk - 1))
            vt_t = a_vtt[kk % 2]
            v.tensor_scalar(out=vt_t[:, 0:64], in0=a_vt[h % 2][:, kc * 64:(kc + 1) * 64],
                            scalar1=a_ncc[:, 2:3], scalar2=None, op0=ALU.mult)
            v.tensor_copy(out=vt_t[:, 64:65], in_=a_ncc[:, 2:3]).then_inc(G("vtS"), 1)
            for p in range(8):
                v.wait_ge(G("pvS"), 8 * kk + p + 1)
                if kc == 0 and p == 0:
                    _w(v, G("attS"), 8 * h)
                pvb = PVB[(8 * kk + p) % 2]
                if kc == 0:
                    ins = v.tensor_copy(out=a_hacc[0:66, p * 512:(p + 1) * 512],
                                        in_=pvb[0:66, 0:512])
                else:
                    ins = v.tensor_tensor(out=a_hacc[0:66, p * 512:(p + 1) * 512],
                                          in0=pvb[0:66, 0:512],
                                          in1=a_hacc[0:66, p * 512:(p + 1) * 512],
                                          op=ALU.add)
                ins.then_inc(G("evS"), 1)
            if kc == NKC - 1:
                for ppn in range(8):
                    e = 8 * h + ppn
                    v.wait_ge(G("ep1"), e + 1)
                    vstep(v.tensor_tensor(out=a_sc[0:1, :], in0=a_sb[0:1, :],
                                          in1=a_tq[0:1, :], op=ALU.mult))
                    vwait()
                    vstep(v.tensor_scalar(out=a_sd[0:1, :], in0=a_sc[0:1, :],
                                          scalar1=1e-12, scalar2=None, op0=ALU.max))
                    vwait()
                    vstep(v.reciprocal(out=a_sc[0:1, :], in_=a_sd[0:1, :]))
                    vwait()
                    v.tensor_tensor(out=a_sd[0:1, :], in0=a_sb[0:1, :],
                                    in1=a_sc[0:1, :], op=ALU.mult).then_inc(G("scl"), 1)
                    v.wait_ge(G("bcS"), e + 1)
                    row0 = 64 * (h % 2)
                    cbase = (h // 2) * S
                    v.tensor_tensor(out=attn_all[row0:row0 + 64,
                                                 cbase + ppn * 512: cbase + (ppn + 1) * 512],
                                    in0=a_hacc[0:64, ppn * 512:(ppn + 1) * 512],
                                    in1=MM[ppn % 2][0:64, 0:512],
                                    op=ALU.mult).then_inc(G("attS"), 1)

        v.wait_ge(G("attS"), 64)
        v.wait_ge(G("evS"), 8 * NCH)
        v.sem_inc(G("bar"), 1)
        if upto in ("attn", "exch"):
            return
        v.wait_ge(G("bar"), 10)
        if upto in ("ffn0p", "ffn0q", "ffn0r", "ffn0u", "ffn0u2", "ffn0u3"):
            nT = {"ffn0p": 1, "ffn0q": 2}.get(upto, 8)
            HS = upto.startswith("ffn0u")
            for i in range(nT):
                v.wait_ge(G("fxT"), i + 1)
                v.tensor_copy(out=f_xfm[:, i * 128:(i + 1) * 128],
                              in_=MM[i % 2][:, 0:128]).then_inc(G("fxfm"), 1)
            return
        if upto in ("ffn0m", "ffn0n"):
            return
        if upto == "ffn0g":
            v.wait_ge(G("fxld"), 16)
            v.tensor_copy(out=f_xfm[:, 0:128], in_=MM[0][:, 0:128])
            return
        if upto in ("gat", "ffn0b", "ffn0d", "ffn0e", "ffn0f"):
            return

        for pf in range(NPF):
            for i8 in range(8 * NTT):
                i = 32 * pf + i8
                tt = i8 // 8
                j = i8 % 8
                v.wait_ge(G("fxT"), i + 1)
                v.tensor_copy(out=f_xfm[:, j * 512 + tt * 128: j * 512 + (tt + 1) * 128],
                              in_=MM[i % 2][:, 0:128]).then_inc(G("fxfm"), 1)
            if FST <= 0:
                continue
            for dc in range(8):
                i = 8 * pf + dc
                v.wait_ge(G("fop"), i + 1)
                if dc == 0 and FST > 4:
                    _w(v, G("fof"), 8 * pf)
                v.scalar_tensor_tensor(out=f_x1p[:, dc * 512:(dc + 1) * 512],
                                       in0=MM[dc % 2][:, 0:512],
                                       scalar=bo_sb[:, dc:dc + 1],
                                       in1=f_xfm[:, dc * 512:(dc + 1) * 512],
                                       op0=ALU.add, op1=ALU.add).then_inc(G("fx1"), 1)
            if FST <= 1:
                continue
            if FSB <= 1:
                continue
            v.wait_ge(G("fstm"), pf + 1)
            vstep(v.tensor_scalar(out=f_st[0:1, :], in0=pv0[0:1, 0:512],
                                  scalar1=1.0 / D, scalar2=None, op0=ALU.mult))
            vstep(v.tensor_scalar(out=f_st2[0:1, :], in0=pv1[0:1, 0:512],
                                  scalar1=1.0 / D, scalar2=None, op0=ALU.mult))
            vwait()
            vstep(v.tensor_tensor(out=f_st3[0:1, :], in0=f_st[0:1, :], in1=f_st[0:1, :],
                                  op=ALU.mult))
            vwait()
            v.tensor_tensor(out=f_st4[0:1, :], in0=f_st2[0:1, :], in1=f_st3[0:1, :],
                            op=ALU.subtract).then_inc(G("fvar"), 1)
            if FSB <= 3:
                continue
            v.wait_ge(G("fbc"), pf + 1)
            for dc in range(8):
                if pf > 0 and dc == 0:
                    v.wait_ge(G("fhmm"), 32 * pf)
                vstep(v.tensor_tensor(out=f_y[:, :], in0=f_x1p[:, dc * 512:(dc + 1) * 512],
                                      in1=pv0[:, 0:512], op=ALU.subtract))
                vwait()
                v.tensor_tensor(out=f_xh2[:, dc * 512:(dc + 1) * 512], in0=f_y[:, :],
                                in1=pv1[:, 0:512], op=ALU.mult).then_inc(G("fxh2"), 2)
            if FST <= 2:
                continue
            for fc in range(32):
                i = 32 * pf + fc
                v.wait_ge(G("fhmm"), i + 1)
                _w(v, G("fhq"), i)
                v.tensor_scalar(out=f_hr[:, :], in0=MM[fc % 2][:, 0:512],
                                scalar1=0.0, scalar2=None, op0=ALU.max).then_inc(G("fhr"), 1)
            if FST <= 3:
                continue
            for dc in range(8):
                i = 8 * pf + dc
                for dcp in ([dc - 1] if dc > 0 and FST > 4 else []):
                    ip = 8 * pf + dcp
                    for tc in range(4):
                        j = 4 * ip + tc
                        v.wait_ge(G("foT"), j + 1)
                        if j >= 4:
                            v.wait_ge(G("fout0" if j % 2 == 0 else "fout1"), 16 * (j // 2 - 1))
                        v.tensor_copy(out=f_os[j % 4][:, :],
                                      in_=MM[j % 2][:, 0:128]).then_inc(G("foev"), 1)
                v.wait_ge(G("fyS"), i + 1)
                pvb = PVB[dc % 2]
                vwait()
                vstep(v.tensor_copy(out=f_y[:, :], in_=pvb[:, 0:512]).then_inc(G("fspk"), 1), "fspk", i + 1)
                vwait()
                vstep(v.tensor_scalar(out=f_spk[:, :], in0=f_y[:, :], scalar1=1.0,
                                      scalar2=None, op0=ALU.is_gt))
                vwait()
                vstep(v.tensor_copy(out=f_acc[:, :], in_=f_spk[:, :]))
                cur = f_y
                bufs = [f_mA, f_mB, f_mC]
                bi = 0
                for st in range(4):
                    tmp = bufs[bi % 3]
                    nxt = bufs[(bi + 1) % 3]
                    bi += 2
                    vwait()
                    vstep(v.scalar_tensor_tensor(out=tmp[:, :], in0=cur[:, :], scalar=0.9,
                                                 in1=f_y[:, :], op0=ALU.mult, op1=ALU.add))
                    vwait()
                    vstep(v.tensor_tensor(out=nxt[:, :], in0=tmp[:, :], in1=f_spk[:, :],
                                          op=ALU.subtract))
                    vwait()
                    vstep(v.tensor_scalar(out=f_spk[:, :], in0=nxt[:, :], scalar1=1.0,
                                          scalar2=None, op0=ALU.is_gt))
                    vwait()
                    vstep(v.tensor_tensor(out=f_acc[:, :], in0=f_acc[:, :], in1=f_spk[:, :],
                                          op=ALU.add))
                    cur = nxt
                if FST > 4:
                    _w(v, G("foT"), 4 * i)
                vwait()
                v.scalar_tensor_tensor(out=f_ofm[:, :], in0=f_acc[:, :], scalar=0.2,
                                       in1=f_x1p[:, dc * 512:(dc + 1) * 512],
                                       op0=ALU.mult, op1=ALU.add).then_inc(G("fof"), 1)
            for dcp in ([7] if FST > 4 else []):
                ip = 8 * pf + dcp
                for tc in range(4):
                    j = 4 * ip + tc
                    v.wait_ge(G("foT"), j + 1)
                    if j >= 4:
                        v.wait_ge(G("fout0" if j % 2 == 0 else "fout1"), 16 * (j // 2 - 1))
                    v.tensor_copy(out=f_os[j % 4][:, :],
                                  in_=MM[j % 2][:, 0:128]).then_inc(G("foev"), 1)

    @block.sync
    def _(sp):
        loads = [
            (ident[:, :], id_in), (eye2[:, :], eye2_in), (blk2[:, :], blk2_in),
            (bq_sb[:, :], bq_in), (bk_sb[:, :], bk_in), (bo_sb[:, :], bo_in),
            (bvrow[:, :], bv_in), (sidx[:, :], sidx_in), (gidx[:, :], gidx_in),
            (fme[:, :], fme_in), (fpr[:, :], fpr_in), (nonce[:, :], non_in),
        ]
        assert len(loads) == NSMALL
        for dst, src in loads:
            sp.dma_start(out=dst, in_=src).then_inc(G("s_small"), 16)

        for i in range(24):
            w_in = [wq_in, wk_in, wv_in][i // 8]
            dc = i % 8
            _w(sp, G("p0wc"), i - 1)
            sp.dma_start(out=p_wst[i % 2][:, :],
                         in_=w_in[dc * 128:(dc + 1) * 128, :]).then_inc(G("p0wld0" if i % 2 == 0 else "p0wld1"), 16)
        def xload(pp):
            _w(sp, G("p0xh"), 4 * (pp - 1))
            sp.dma_start(out=p_xst[pp % 2][:, :].rearrange("p (t d) -> p t d", t=4),
                         in_=x_in[pp * 512:(pp + 1) * 512, :].rearrange("(t p) d -> p t d", p=128)
                         ).then_inc(G("p0x0" if pp % 2 == 0 else "p0x1"), 16)

        xload(0)
        xload(1)
        for pp in range(8):
            for gr in range(12):
                si = 12 * pp + gr
                stage = p_stage[si % 2]
                sp.wait_ge(G("p0st"), si + 1)
                if gr < 8:
                    dstt = qs if gr < 4 else ks
                    fc = gr % 4
                    ins = sp.dma_start(out=dstt.ap()[fc * 128:(fc + 1) * 128,
                                                     pp * 512:(pp + 1) * 512],
                                       in_=stage[:, :])
                else:
                    kcg = 4 * pp + (gr - 8)
                    ins = sp.dma_start(out=vts.ap()[kcg, :, :], in_=stage[:, :])
                ins.then_inc(G("p0sd0" if si % 2 == 0 else "p0sd1"), 16)
            if pp + 2 < 8:
                xload(pp + 2)
        sp.wait_ge(G("p0sd0"), 16 * 48)
        sp.wait_ge(G("p0sd1"), 16 * 48)
        sp.sem_inc(G("bar"), 1)
        sp.wait_ge(G("bar"), 5)
        if upto == "p0":
            return

        def head_load(h):
            hs = "hload0" if h % 2 == 0 else "hload1"
            sp.dma_start(out=a_qh[h % 2][:, :],
                         in_=qs.ap()[h * 64:(h + 1) * 64, :]).then_inc(G(hs), 16)
            sp.dma_start(out=a_kh[h % 2][:, :],
                         in_=ks.ap()[h * 64:(h + 1) * 64, :]).then_inc(G(hs), 16)
            sp.dma_start(out=a_vt[h % 2][:, :].rearrange("p (k f) -> p k f", k=NKC),
                         in_=vts.ap()[:, :, h * 64:(h + 1) * 64].rearrange("k p f -> p k f")
                         ).then_inc(G(hs), 16)

        head_load(0)
        head_load(1)
        for h in range(HL):
            sp.wait_ge(G("evS"), 256 * (h + 1))
            for ppn in range(8):
                e = 8 * h + ppn
                _w(sp, G("ep1"), e)
                _w(sp, G("scl"), e)
                sp.dma_start(out=a_nr[0:1, :],
                             in_=a_hacc[65:66, ppn * 512:(ppn + 1) * 512]).then_inc(G("stg"), 16)
                sp.dma_start(out=a_tq[0:1, :],
                             in_=a_hacc[64:65, ppn * 512:(ppn + 1) * 512]).then_inc(G("stg"), 16)
            if h + 2 < HL:
                sp.wait_ge(G("dot"), 128 * (h + 1))
                sp.wait_ge(G("vtS"), 32 * (h + 1))
                head_load(h + 2)
        if upto == "dump":
            sp.wait_ge(G("attS"), 64)
            sp.dma_start(out=dbg_q, in_=qs.ap()).then_inc(G("fout0"), 16)
            sp.dma_start(out=dbg_k, in_=ks.ap()).then_inc(G("fout0"), 16)
            sp.dma_start(out=dbg_a, in_=attn_all[:, :]).then_inc(G("fout0"), 16)
            sp.dma_start(out=dbg_h, in_=a_hacc[:, :]).then_inc(G("fout0"), 16)
            sp.dma_start(out=dbg_u, in_=a_B1[:, :]).then_inc(G("fout0"), 16)
            sp.dma_start(out=dbg_s, in_=a_s[:, :]).then_inc(G("fout0"), 16)
            sp.dma_start(out=dbg_n, in_=a_ncc[:, :]).then_inc(G("fout0"), 16)
            sp.wait_ge(G("fout0"), 112)
        sp.wait_ge(G("stg"), 32 * 64)
        sp.sem_inc(G("bar"), 1)
        if upto in ("attn", "exch"):
            return
        sp.wait_ge(G("bar"), 10)
        if upto == "gat":
            return
        if upto == "ffn0e":
            sp.dma_start(out=fval[:, :], in_=non_in).then_inc(G("fxld"), 16)
            sp.wait_ge(G("fxld"), 16)
            return
        if upto in ("ffn0f", "ffn0g"):
            sp.dma_start(out=f_x[:, :].rearrange("p (t d) -> p t d", t=4),
                         in_=x_in[0:512, :].rearrange("(t p) d -> p t d", p=128)
                         ).then_inc(G("fxld"), 16)
            sp.wait_ge(G("fxld"), 16)
            return
        if upto == "ffn0d":
            return

        for pf in range(NPF):
            _w(sp, G("fxT"), 32 * pf)
            if upto == "ffn0i":
                sp.sem_inc(G("fxld"), 16)
                continue
            sp.dma_start(out=f_x[:, :].rearrange("p (t d) -> p t d", t=4),
                         in_=x_in[pf * 512:(pf + 1) * 512, :].rearrange("(t p) d -> p t d", p=128)
                         ).then_inc(G("fxld"), 16)
            if FST <= 4:
                continue
            for dc in range(8):
                for tc in range(4):
                    j = 32 * pf + 4 * dc + tc
                    sp.wait_ge(G("foev"), j + 1)
                    sp.dma_start(out=out_o[pf * 512 + tc * 128: pf * 512 + (tc + 1) * 128,
                                           dc * 128:(dc + 1) * 128],
                                 in_=f_os[j % 4][:, :]).then_inc(G("fout0" if j % 2 == 0 else "fout1"), 16)
        if FST > 4:
            sp.wait_ge(G("fout0"), 16 * 16 * NPF)
            sp.wait_ge(G("fout1"), 16 * 16 * NPF)

    @block.gpsimd
    def _(gp):
        gp.sem_inc(G("bar"), 1)
        gp.wait_ge(G("bar"), 5)
        if upto == "p0":
            return
        for i in range(NWT):
            kind, r, c, src = cast_tiles[i]
            if i >= 2:
                gp.wait_ge(G("wcdn"), i - 1)
            gp.dma_start(out=a_wc[i % 2][:, :], in_=src).then_inc(G("wcld0" if i % 2 == 0 else "wcld1"), 16)
            iw = i - 1
            if iw >= NWO:
                kindw, rw, cw, _s = cast_tiles[iw]
                dstt = wfcb if kindw == "wfc" else wpjb
                gp.wait_ge(G("wcdn"), iw + 1)
                oi = iw - NWO
                gp.dma_start(out=dstt.ap()[rw * 128:(rw + 1) * 128, cw * 512:(cw + 1) * 512],
                             in_=a_wb[iw % 2][:, :]).then_inc(G("wcod0" if oi % 2 == 0 else "wcod1"), 16)
        iw = NWT - 1
        kindw, rw, cw, _s = cast_tiles[iw]
        gp.wait_ge(G("wcdn"), iw + 1)
        oi = iw - NWO
        gp.dma_start(out=(wfcb if kindw == "wfc" else wpjb).ap()[rw * 128:(rw + 1) * 128,
                                                                cw * 512:(cw + 1) * 512],
                     in_=a_wb[iw % 2][:, :]).then_inc(G("wcod0" if oi % 2 == 0 else "wcod1"), 16)
        if upto == "attn":
            gp.sem_inc(G("bar"), 1)
            return
        for h in range(HL):
            gp.wait_ge(G("attS"), 8 * (h + 1))
            row0 = 64 * (h % 2)
            cbase = (h // 2) * S
            gp.indirect_dma_start(
                shd.ap(), bass.IndirectOffsetOnAxis(sidx[:, h:h + 1], 0),
                attn_all[row0:row0 + 64, cbase + TM: cbase + S], None,
            ).then_inc(G("exs"), 16)
        gp.wait_ge(G("exs"), 16 * HL)
        gp.wait_ge(G("wcod0"), 16 * ((NWT - NWO) // 2))
        gp.wait_ge(G("wcod1"), 16 * ((NWT - NWO) // 2))
        gp.indirect_dma_start(
            shf.ap(), bass.IndirectOffsetOnAxis(fme[:, 0:1], 0), nonce[:, :], None,
        ).then_inc(G("exs"), 16)
        gp.wait_ge(G("exs"), 16 * (HL + 1))
        with gp.register("cnt") as cnt, gp.register("nr_") as nr_, gp.register("df") as df:
            nv = gp.value_load(nonce[0:1, 0:1])
            gp.reg_mov(nr_, nv)
            gp.reg_mov(cnt, 0)

            def cond():
                gp.reg_add(cnt, cnt, 16)
                gp.indirect_dma_start(
                    fval[:, :], None, shf.ap(),
                    bass.IndirectOffsetOnAxis(fpr[:, 0:1], 0),
                ).then_inc(G("pls"), 16)
                gp.wait_ge(G("pls"), cnt)
                vv = gp.value_load(fval[0:1, 0:1])
                gp.reg_sub(df, vv, nr_)
                return df

            with gp.While(cond):
                gp.nop()
        gp.sem_inc(G("bar"), 1)
        if upto == "exch":
            return
        gp.wait_ge(G("bar"), 10)
        if upto in ("ffn0f", "ffn0g", "ffn0m", "ffn0n", "ffn0p", "ffn0q", "ffn0r", "ffn0u", "ffn0u2", "ffn0u3"):
            return
        if upto == "ffn0d":
            gp.dma_start(out=f_x[:, :].rearrange("p (t d) -> p t d", t=4),
                         in_=x_in[0:512, :].rearrange("(t p) d -> p t d", p=128)
                         ).then_inc(G("fxld"), 16)
            gp.wait_ge(G("fxld"), 16)
            return
        if upto == "ffn0a":
            return
        for pf in range(2 if upto == "gat" else NPF):
            _w(gp, G("fop"), 8 * (pf - 1))
            for fc in range(4):
                gp.indirect_dma_start(
                    f_pap[pf % 2][:, fc * 512:(fc + 1) * 512], None,
                    shd.ap(),
                    bass.IndirectOffsetOnAxis(gidx[:, fc:fc + 1], 0),
                    element_offset=pf * 512,
                ).then_inc(G("gat0" if pf % 2 == 0 else "gat1"), 16)
            if upto == "gat" or FST <= 2:
                continue
            for fc in range(32):
                i = 32 * pf + fc
                _w(gp, G("fhmm"), i - 1)
                gp.dma_start(out=f_w1[i % 2][:, :].rearrange("p (a c) -> p a c", a=8),
                             in_=wfcb.ap()[:, fc * 128:(fc + 1) * 128]
                             .rearrange("(a p) c -> p a c", p=128)).then_inc(G("fw1a" if i % 2 == 0 else "fw1b"), 16)
            if FST <= 3:
                continue
            for dc in range(8):
                i = 8 * pf + dc
                _w(gp, G("fyS"), i - 1)
                gp.dma_start(out=f_w2[i % 2][:, :].rearrange("p (a c) -> p a c", a=32),
                             in_=wpjb.ap()[:, dc * 128:(dc + 1) * 128]
                             .rearrange("(a p) c -> p a c", p=128)).then_inc(G("fw2a" if i % 2 == 0 else "fw2b"), 16)

    return nc, stack


_NC_CACHE = {}


def _get_nc():
    if "nc" not in _NC_CACHE:
        _NC_CACHE["nc"] = build_nc()
    return _NC_CACHE["nc"]


def make_in_maps(L0_attn_x, L0_attn_wq, L0_attn_bq, L0_attn_wk, L0_attn_bk,
                 L0_attn_wv, L0_attn_bv, L0_attn_wo, L0_attn_bo,
                 L0_ffn_W_fc, L0_ffn_W_proj):
    import secrets
    x = np.ascontiguousarray(np.asarray(L0_attn_x, dtype=np.float32))
    wq = np.ascontiguousarray(np.asarray(L0_attn_wq, dtype=np.float32))
    wk = np.ascontiguousarray(np.asarray(L0_attn_wk, dtype=np.float32))
    wv = np.ascontiguousarray(np.asarray(L0_attn_wv, dtype=np.float32))
    bq = np.asarray(L0_attn_bq, dtype=np.float32)
    bk = np.asarray(L0_attn_bk, dtype=np.float32)
    bv = np.asarray(L0_attn_bv, dtype=np.float32)
    wo = np.ascontiguousarray(np.asarray(L0_attn_wo, dtype=np.float32))
    bo = np.asarray(L0_attn_bo, dtype=np.float32)
    wfcT = np.ascontiguousarray(np.asarray(L0_ffn_W_fc, dtype=np.float32).T)
    wpjT = np.ascontiguousarray(np.asarray(L0_ffn_W_proj, dtype=np.float32).T)

    identity = np.eye(128, dtype=np.float32)
    eye2 = np.zeros((128, 2), dtype=np.float32)
    eye2[0:64, 0] = 1.0
    eye2[64:128, 1] = 1.0
    blk2 = np.zeros((2, 128), dtype=np.float32)
    blk2[0, 0:64] = 1.0
    blk2[1, 64:128] = 1.0
    nonce_val = (secrets.randbits(30) | 1)

    in_maps = []
    for cid in range(8):
        b = cid // 2
        hg = cid % 2
        pr = cid ^ 1
        perm = np.concatenate([np.arange(hg * TM, (hg + 1) * TM),
                               np.arange((1 - hg) * TM, (2 - hg) * TM) % S])
        fsl = slice(hg * FQ, (hg + 1) * FQ)
        wo_perm = np.concatenate([wo[hg * FQ:(hg + 1) * FQ, :],
                                  wo[(1 - hg) * FQ:(2 - hg) * FQ, :]], axis=0)
        m = {
            "x_in": np.ascontiguousarray(x[b][perm]),
            "wq_in": np.ascontiguousarray(wq[:, fsl]),
            "wk_in": np.ascontiguousarray(wk[:, fsl]),
            "wv_in": np.ascontiguousarray(wv[:, fsl]),
            "bq_in": np.ascontiguousarray(bq[fsl].reshape(4, 128).T),
            "bk_in": np.ascontiguousarray(bk[fsl].reshape(4, 128).T),
            "bv_in": np.ascontiguousarray(bv[fsl].reshape(1, FQ)),
            "wo_in": np.ascontiguousarray(wo_perm),
            "bo_in": np.ascontiguousarray(bo.reshape(8, 128).T),
            "wfc_in": wfcT,
            "wpj_in": wpjT,
            "id_in": identity,
            "eye2_in": eye2,
            "blk2_in": blk2,
            "sidx_in": np.ascontiguousarray(
                (cid * FQ + np.arange(64)[:, None] + 64 * (np.arange(8)[None, :] % 2)
                 + 128 * (np.arange(8)[None, :] // 2)).astype(np.int32)),
            "gidx_in": np.ascontiguousarray(
                (pr * FQ + np.arange(128)[:, None] + 128 * np.arange(4)[None, :]).astype(np.int32)),
            "fme_in": np.array([[2 * cid], [2 * cid + 1]], dtype=np.int32),
            "fpr_in": np.array([[2 * pr], [2 * pr + 1]], dtype=np.int32),
            "non_in": np.full((2, 16), nonce_val, dtype=np.int32),
        }
        in_maps.append(m)
    return in_maps


def assemble_out(results):
    out = np.empty((4, S, D), dtype=np.float32)
    for cid in range(8):
        b = cid // 2
        hg = cid % 2
        out[b, hg * TM:(hg + 1) * TM, :] = results[cid]["out_o"]
    return out


def kernel(**inputs):
    in_maps = make_in_maps(**inputs)
    nc, _stack = _get_nc()
    res = run_bass_kernel_spmd(nc, in_maps, core_ids=list(range(8)))
    return assemble_out([res.results[c] for c in range(8)])


# version 06 (final)
# speedup vs baseline: 1.3501x; 1.2248x over previous
import numpy as np
import concourse.bass as bass
import concourse.mybir as mybir
from concourse.bass_utils import run_bass_kernel_spmd

F32 = mybir.dt.float32
BF16 = mybir.dt.bfloat16
I32 = mybir.dt.int32
AF = mybir.ActivationFunctionType
ALU = mybir.AluOpType

S = 4096
D = 1024
DH = 64
HL = 8
FQ = 512
NKC = 32
NCH = HL * NKC
TM = 2048
FF = 4096
CLIPV = 1.0 - 1e-6
HPI1 = 1.0 + np.pi / 2.0
NWO = 16
NWT = 144
NSMALL = 12


def _w(eng, sem, v):
    if v >= 1:
        eng.wait_ge(sem, v)


def build_nc(upto="full"):
    NPF = 1 if upto.startswith("ffn") else 4
    FSB = {"ffnb1": 1, "ffnb2a": 15, "ffnb2x": 16, "ffnb2y": 17, "ffnb2": 2, "ffnb3": 3}.get(upto, 99)
    FST = {"ffn0": 0, "ffn0a": 0, "ffn0b": 0, "ffn0d": -1, "ffn0e": -1, "ffn0f": -1, "ffn0g": -1,
           "ffn0h": 0, "ffn0i": 0, "ffn0m": -1, "ffn0n": -1, "ffn0p": -1, "ffn0q": -1, "ffn0r": -1, "ffn0u": -1, "ffn0u2": -1, "ffn0u3": -1, "ffna": 1, "ffnb1": 2, "ffnb2": 2, "ffnb2a": 2, "ffnb2x": 2, "ffnb2y": 2, "ffnb3": 2, "ffnb": 2, "ffnc": 3, "ffnd": 4}.get(upto, 9)
    NTT = 1 if upto in ("ffn0h", "ffn0i") else 4
    nc = bass.Bass("TRN2", target_bir_lowering=False, debug=False)

    x_in = nc.dram_tensor("x_in", [S, D], F32, kind="ExternalInput").ap()
    wq_in = nc.dram_tensor("wq_in", [D, FQ], F32, kind="ExternalInput").ap()
    wk_in = nc.dram_tensor("wk_in", [D, FQ], F32, kind="ExternalInput").ap()
    wv_in = nc.dram_tensor("wv_in", [D, FQ], F32, kind="ExternalInput").ap()
    bq_in = nc.dram_tensor("bq_in", [128, 4], F32, kind="ExternalInput").ap()
    bk_in = nc.dram_tensor("bk_in", [128, 4], F32, kind="ExternalInput").ap()
    bv_in = nc.dram_tensor("bv_in", [1, FQ], F32, kind="ExternalInput").ap()
    wo_in = nc.dram_tensor("wo_in", [D, D], F32, kind="ExternalInput").ap()
    bo_in = nc.dram_tensor("bo_in", [128, 8], F32, kind="ExternalInput").ap()
    wfc_in = nc.dram_tensor("wfc_in", [D, FF], F32, kind="ExternalInput").ap()
    wpj_in = nc.dram_tensor("wpj_in", [FF, D], F32, kind="ExternalInput").ap()
    id_in = nc.dram_tensor("id_in", [128, 128], F32, kind="ExternalInput").ap()
    eye2_in = nc.dram_tensor("eye2_in", [128, 2], F32, kind="ExternalInput").ap()
    blk2_in = nc.dram_tensor("blk2_in", [2, 128], F32, kind="ExternalInput").ap()
    sidx_in = nc.dram_tensor("sidx_in", [64, 8], I32, kind="ExternalInput").ap()
    gidx_in = nc.dram_tensor("gidx_in", [128, 4], I32, kind="ExternalInput").ap()
    fme_in = nc.dram_tensor("fme_in", [2, 1], I32, kind="ExternalInput").ap()
    fpr_in = nc.dram_tensor("fpr_in", [2, 1], I32, kind="ExternalInput").ap()
    non_in = nc.dram_tensor("non_in", [2, 16], I32, kind="ExternalInput").ap()
    out_o = nc.dram_tensor("out_o", [TM, D], F32, kind="ExternalOutput").ap()
    dbg_q = nc.dram_tensor("dbg_q", [FQ, S], BF16, kind="ExternalOutput").ap()
    dbg_k = nc.dram_tensor("dbg_k", [FQ, S], BF16, kind="ExternalOutput").ap()
    dbg_a = nc.dram_tensor("dbg_a", [128, 4 * S], BF16, kind="ExternalOutput").ap()
    dbg_h = nc.dram_tensor("dbg_h", [128, S], F32, kind="ExternalOutput").ap()
    dbg_u = nc.dram_tensor("dbg_u", [128, S], F32, kind="ExternalOutput").ap()
    dbg_s = nc.dram_tensor("dbg_s", [128, S], BF16, kind="ExternalOutput").ap()
    dbg_n = nc.dram_tensor("dbg_n", [128, 4], F32, kind="ExternalOutput").ap()

    qs = nc.dram_tensor("qs", [FQ, S], BF16, kind="Internal")
    ks = nc.dram_tensor("ks", [FQ, S], BF16, kind="Internal")
    vts = nc.dram_tensor("vts", [NKC, 128, FQ], BF16, kind="Internal")
    wfcb = nc.dram_tensor("wfcb", [D, FF], BF16, kind="Internal")
    wpjb = nc.dram_tensor("wpjb", [FF, D], BF16, kind="Internal")
    shd = nc.dram_tensor("shd", [8 * FQ, TM], BF16, kind="Internal", addr_space="Shared")
    shf = nc.dram_tensor("shf", [16, 16], I32, kind="Internal", addr_space="Shared")

    A = nc.alloc_sbuf_tensor
    consts = A("consts", [128, 8], F32)
    ident = A("ident", [128, 128], F32)
    eye2 = A("eye2", [128, 2], F32)
    blk2 = A("blk2", [2, 128], F32)
    onesrow = A("onesrow", [1, 128], F32)
    onescol = A("onescol", [128, 1], F32)
    ones2 = A("ones2", [128, 2], F32)
    bq_sb = A("bq_sb", [128, 4], F32)
    bk_sb = A("bk_sb", [128, 4], F32)
    bo_sb = A("bo_sb", [128, 8], F32)
    bvrow = A("bvrow", [1, FQ], F32)
    bvbc = A("bvbc", [128, FQ], F32)
    sidx = A("sidx", [64, 8], I32)
    gidx = A("gidx", [128, 4], I32)
    fme = A("fme", [2, 1], I32)
    fpr = A("fpr", [2, 1], I32)
    nonce = A("nonce", [2, 16], I32)
    fval = A("fval", [2, 16], I32)
    wo_sb = A("wo_sb", [128, 8 * D], BF16)

    used = nc.SBUF_PARTITION_SIZE_BYTES - nc.sbuf_bytes_remaining
    AB = ((used + 255) // 256) * 256 + 256
    CAP = 229344

    def ar(name, shape, dtype, off):
        nbytes = shape[-1] * mybir.dt.size(dtype)
        assert AB + off + nbytes <= CAP, (name, AB, off, nbytes)
        return nc.alloc_sbuf_tensor_at(name, shape, dtype, offset=AB + off)

    attn_all = ar("attn_all", [128, 4 * S], BF16, 0)

    p_xst = [ar("p_xst0", [128, 4096], F32, 0), ar("p_xst1", [128, 4096], F32, 16384)]
    p_xhat = ar("p_xhat", [128, 4096], F32, 32768)
    p_xhT = ar("p_xhT", [128, 4096], BF16, 49152)
    p_wqsb = ar("p_wqsb", [128, 4096], BF16, 57344)
    p_wksb = ar("p_wksb", [128, 4096], BF16, 65536)
    p_wvsb = ar("p_wvsb", [128, 4096], BF16, 73728)
    p_wst = [ar("p_wst0", [128, 512], F32, 81920), ar("p_wst1", [128, 512], F32, 83968)]
    p_ybuf = ar("p_ybuf", [128, 512], F32, 86016)
    p_ysq = ar("p_ysq", [128, 512], F32, 88064)
    p_stage = [ar("p_stage0", [128, 512], BF16, 90112), ar("p_stage1", [128, 512], BF16, 91136)]
    p_stat = ar("p_stat", [128, 8], F32, 92160)
    p_nrm = ar("p_nrm", [2, 512], F32, 92416)
    p_inv = ar("p_inv", [2, 512], F32, 94464)
    p_sqd = ar("p_sqd", [128, 1024], F32, 96512)

    a_qh = [ar("a_qh0", [64, S], BF16, 32768), ar("a_qh1", [64, S], BF16, 40960)]
    a_kh = [ar("a_kh0", [64, S], BF16, 49152), ar("a_kh1", [64, S], BF16, 57344)]
    a_vt = [ar("a_vt0", [128, NKC * DH], BF16, 65536), ar("a_vt1", [128, NKC * DH], BF16, 69632)]
    a_B0 = [ar("a_B00", [128, S], F32, 73728), ar("a_B01", [128, S], F32, 90112)]
    a_B1 = ar("a_B1", [128, S], F32, 106496)
    a_B2 = ar("a_B2", [128, S], F32, 122880)
    a_s = ar("a_s", [128, S], BF16, 139264)
    a_hacc = ar("a_hacc", [128, S], F32, 147456)
    a_vtt = [ar("a_vtt0", [128, 66], BF16, 163840), ar("a_vtt1", [128, 66], BF16, 164096)]
    a_ncc = ar("a_ncc", [128, 4], F32, 164352)
    a_nr = ar("a_nr", [1, 512], F32, 164608)
    a_tq = ar("a_tq", [1, 512], F32, 166656)
    a_sa = ar("a_sa", [1, 512], F32, 168704)
    a_sb = ar("a_sb", [1, 512], F32, 170752)
    a_sc = ar("a_sc", [1, 512], F32, 172800)
    a_sd = ar("a_sd", [1, 512], F32, 174848)
    a_wc = [ar("a_wc0", [128, 512], F32, 176896), ar("a_wc1", [128, 512], F32, 178944)]
    a_wb = [ar("a_wb0", [128, 512], BF16, 180992), ar("a_wb1", [128, 512], BF16, 182016)]

    f_x = ar("f_x", [128, 4096], F32, 32768)
    f_xfm = ar("f_xfm", [128, 4096], F32, 49152)
    f_x1p = ar("f_x1p", [128, 4096], F32, 65536)
    f_xh2 = ar("f_xh2", [128, 4096], BF16, 81920)
    f_xsq = ar("f_xsq", [128, 512], F32, 90112)
    f_h = ar("f_h", [128, 16384], BF16, 92160)
    f_w1 = [ar("f_w10", [128, 1024], BF16, 124928), ar("f_w11", [128, 1024], BF16, 126976)]
    f_w2 = [ar("f_w20", [128, 4096], BF16, 129024), ar("f_w21", [128, 4096], BF16, 137216)]
    f_pap = [ar("f_pap0", [128, 2048], BF16, 145408), ar("f_pap1", [128, 2048], BF16, 149504)]
    f_y = ar("f_y", [128, 512], F32, 153600)
    f_mA = ar("f_mA", [128, 512], F32, 155648)
    f_mB = ar("f_mB", [128, 512], F32, 157696)
    f_mC = ar("f_mC", [128, 512], F32, 159744)
    f_spk = ar("f_spk", [128, 512], F32, 161792)
    f_acc = ar("f_acc", [128, 512], F32, 163840)
    f_hr = [ar("f_hr0", [128, 512], F32, 165888), ar("f_hr1", [128, 512], F32, 174080)]
    f_ofm = ar("f_ofm", [128, 512], F32, 167936)
    f_os = [ar("f_os0", [128, 128], F32, 169984), ar("f_os1", [128, 128], F32, 170496),
            ar("f_os2", [128, 128], F32, 171008), ar("f_os3", [128, 128], F32, 171520)]
    f_st = ar("f_st", [2, 512], F32, 178176)
    f_st2 = ar("f_st2", [2, 512], F32, 180224)
    f_st3 = ar("f_st3", [1, 512], F32, 182272)
    f_st4 = ar("f_st4", [1, 512], F32, 184320)

    psA = nc.alloc_psum_tensor("psA", [128, 1024], F32)
    psB = nc.alloc_psum_tensor("psB", [128, 1024], F32)
    pv0 = nc.alloc_psum_tensor("pv0", [128, 512], F32)
    pv1 = nc.alloc_psum_tensor("pv1", [128, 512], F32)
    mm0 = nc.alloc_psum_tensor("mm0", [128, 512], F32)
    mm1 = nc.alloc_psum_tensor("mm1", [128, 512], F32)
    PVB = [pv0, pv1]
    MM = [mm0, mm1]

    semnames = [
        "s_small", "s_dve", "s_bv",
        "p0wld0", "p0wld1", "p0wc", "p0x0", "p0x1", "p0sq", "p0var", "p0rstd", "p0xh",
        "p0T", "p0Tev", "p0mm", "p0yb", "p0ysq", "p0nrmm", "p0sqr", "p0inv",
        "p0bc", "p0fin", "p0st", "p0sd0", "p0sd1",
        "bar",
        "hload0", "hload1", "dot", "clip", "lnS", "dS", "sS", "cS", "vtS", "pvS", "evS",
        "stg", "ep1", "scl", "bcS", "attS",
        "wcld0", "wcld1", "wcdn", "wcod0", "wcod1", "wcst",
        "exs", "pls", "gat0", "gat1",
        "fxld", "fxT", "fxfm", "fop", "fx1", "fstq", "fsq2", "fstm", "fvar",
        "frst", "fstc", "fbc", "fxh2", "fw1a", "fw1b", "fhmm", "fhr", "fhq", "fw2a", "fw2b", "fyS",
        "fspk", "fof", "foT", "foev", "fout0", "fout1", "vch", "sch",
    ]
    from contextlib import ExitStack
    stack = ExitStack()
    sems = {n: stack.enter_context(nc.semaphore(n)) for n in semnames}
    G = sems.__getitem__

    cast_tiles = []
    for i in range(NWO):
        r, c = i // 2, i % 2
        cast_tiles.append(("wo", r, c, wo_in[r * 128:(r + 1) * 128, c * 512:(c + 1) * 512]))
    for j in range(64):
        r, c = j // 8, j % 8
        cast_tiles.append(("wfc", r, c, wfc_in[r * 128:(r + 1) * 128, c * 512:(c + 1) * 512]))
    for j in range(64):
        r, c = j // 2, j % 2
        cast_tiles.append(("wpj", r, c, wpj_in[r * 128:(r + 1) * 128, c * 512:(c + 1) * 512]))

    block = stack.enter_context(nc.Block())

    @block.tensor
    def _(t):
        t.wait_ge(G("s_small"), 16 * NSMALL)
        t.wait_ge(G("s_dve"), 1)
        t.matmul(mm0[:, 0:512], onesrow[0:1, :], bvrow[0:1, :], start=True, stop=True).then_inc(G("p0bc"), 1)

        for pp in range(8):
            _w(t, G("p0st"), 12 * pp)
            for tt in range(4):
                gi = 4 * pp + tt
                _w(t, G("p0xh"), gi + 1)
                for j in range(8):
                    i = 8 * gi + j
                    _w(t, G("p0Tev"), i - 1)
                    t.transpose(MM[i % 2][:, 0:128],
                                p_xhat[:, tt * 1024 + j * 128: tt * 1024 + (j + 1) * 128],
                                ident[:, :]).then_inc(G("p0T"), 1)
            if pp == 0:
                t.wait_ge(G("p0wc"), 24)
            _w(t, G("p0Tev"), 32 * (pp + 1))
            for gr in range(12):
                gg = 12 * pp + gr
                pv = PVB[gr % 2]
                gp2 = gg - 2
                if gp2 >= 0:
                    p2, g2 = divmod(gp2, 12)
                    if g2 < 8:
                        _w(t, G("p0yb"), 8 * p2 + g2 + 1)
                    else:
                        _w(t, G("p0st"), gp2 + 1)
                if gr < 8:
                    w_sb = p_wqsb if gr < 4 else p_wksb
                    fc = gr % 4
                    for dc in range(8):
                        ins = t.matmul(pv[:, 0:512],
                                       w_sb[:, dc * 512 + fc * 128: dc * 512 + (fc + 1) * 128],
                                       p_xhT[:, dc * 512:(dc + 1) * 512],
                                       start=(dc == 0), stop=(dc == 7))
                else:
                    tc = gr - 8
                    for dc in range(8):
                        ins = t.matmul(pv[:, 0:512],
                                       p_xhT[:, dc * 512 + tc * 128: dc * 512 + (tc + 1) * 128],
                                       p_wvsb[:, dc * 512:(dc + 1) * 512],
                                       start=(dc == 0), stop=(dc == 7))
                ins.then_inc(G("p0mm"), 1)
                for grn in ([gr - 1] if 1 <= gr <= 8 else []):
                    gg2 = 8 * pp + grn
                    mm = MM[grn % 2]
                    _w(t, G("p0ysq"), gg2 + 1)
                    _w(t, G("p0st"), 12 * pp + grn - 1)
                    t.matmul(mm[0:2, 0:512], eye2[:, :], p_ysq[:, 0:512],
                             start=True, stop=True).then_inc(G("p0nrmm"), 1)
                    _w(t, G("p0inv"), gg2 + 1)
                    t.matmul(mm[:, 0:512], blk2[:, :], p_inv[0:2, 0:512],
                             start=True, stop=True).then_inc(G("p0bc"), 1)

        t.wait_ge(G("p0bc"), 65)
        t.sem_inc(G("bar"), 1)
        t.wait_ge(G("bar"), 5)
        if upto == "p0":
            return

        def emit_pv(kk):
            _w(t, G("vtS"), kk + 1)
            for p in range(8):
                pvb = PVB[(8 * kk + p) % 2]
                _w(t, G("evS"), 8 * kk + p - 1)
                t.matmul(pvb[0:66, 0:512], a_vtt[kk % 2][:, :],
                         a_s[:, p * 512:(p + 1) * 512],
                         start=True, stop=True).then_inc(G("pvS"), 1)

        def emit_epi_bc(h):
            for ppn in range(8):
                e = 8 * h + ppn
                _w(t, G("scl"), e + 1)
                _w(t, G("attS"), e - 1)
                t.matmul(MM[ppn % 2][0:64, 0:512], onesrow[0:1, 0:64], a_sd[0:1, :],
                         start=True, stop=True).then_inc(G("bcS"), 1)

        for kk in range(NCH):
            h, kc = divmod(kk, NKC)
            if kc == 0:
                t.wait_ge(G("hload0" if h % 2 == 0 else "hload1"), 48 * (h // 2 + 1))
            for q in range(4):
                Q = 4 * kk + q
                _w(t, G("clip"), Q - 1)
                pb = psA if q % 2 == 0 else psB
                t.matmul(pb[:, 0:512], a_kh[h % 2][0:64, kc * 128:(kc + 1) * 128],
                         a_qh[h % 2][0:64, q * 1024: q * 1024 + 512],
                         start=True, stop=True)
                t.matmul(pb[:, 512:1024], a_kh[h % 2][0:64, kc * 128:(kc + 1) * 128],
                         a_qh[h % 2][0:64, q * 1024 + 512: (q + 1) * 1024],
                         start=True, stop=True).then_inc(G("dot"), 1)
            if kk > 0:
                emit_pv(kk - 1)
            if kc == 0 and kk > 0:
                emit_epi_bc(h - 1)
        emit_pv(NCH - 1)
        emit_epi_bc(HL - 1)

        t.wait_ge(G("bcS"), 64)
        t.wait_ge(G("pvS"), 8 * NCH)
        t.sem_inc(G("bar"), 1)
        if upto in ("attn", "exch"):
            return
        t.wait_ge(G("bar"), 10)
        if upto == "ffn0m":
            t.wait_ge(G("wcst"), 1)
            return
        if upto in ("ffn0p", "ffn0q", "ffn0r", "ffn0u", "ffn0u2", "ffn0u3"):
            nT = {"ffn0p": 1, "ffn0q": 2}.get(upto, 8)
            HS = upto.startswith("ffn0u")
            t.wait_ge(G("wcst"), 1)
            t.wait_ge(G("fxld"), 16)
            for i in range(nT):
                if HS:
                    _w(t, G("fxfm"), i - 1)
                t.transpose(MM[i % 2][:, 0:128], f_x[:, i * 128:(i + 1) * 128],
                            ident[:, :]).then_inc(G("fxT"), 1)
            return
        if upto == "ffn0n":
            t.wait_ge(G("wcdn"), NWT)
            return
        if upto == "ffn0g":
            t.wait_ge(G("fxld"), 16)
            t.transpose(MM[0][:, 0:128], f_x[:, 0:128], ident[:, :])
            return
        if upto in ("gat", "ffn0d", "ffn0e", "ffn0f"):
            return

        t.wait_ge(G("wcst"), 1)

        def emit_oT(i):
            for tc in range(4):
                j = 4 * i + tc
                _w(t, G("foev"), j - 1)
                t.transpose(MM[j % 2][:, 0:128], f_ofm[:, tc * 128:(tc + 1) * 128],
                            ident[:, :]).then_inc(G("foT"), 1)

        for pf in range(NPF):
            t.wait_ge(G("fxld"), 16 * (pf + 1))
            for tt in range(NTT):
                for j in range(8):
                    i = 32 * pf + 8 * tt + j
                    _w(t, G("fxfm"), i - 1)
                    t.transpose(MM[i % 2][:, 0:128],
                                f_x[:, tt * 1024 + j * 128: tt * 1024 + (j + 1) * 128],
                                ident[:, :]).then_inc(G("fxT"), 1)
            if FST <= 0:
                continue
            t.wait_ge(G("gat0" if pf % 2 == 0 else "gat1"), 64 * (pf // 2 + 1))
            t.wait_ge(G("fxfm"), 32 * (pf + 1))
            for dc in range(8):
                i = 8 * pf + dc
                _w(t, G("fx1"), i - 1)
                for fc in range(8):
                    if fc < 4:
                        rhs = attn_all[:, fc * S + pf * 512: fc * S + pf * 512 + 512]
                    else:
                        rhs = f_pap[pf % 2][:, (fc - 4) * 512:(fc - 3) * 512]
                    ins = t.matmul(MM[dc % 2][:, 0:512],
                                   wo_sb[:, fc * D + dc * 128: fc * D + (dc + 1) * 128],
                                   rhs, start=(fc == 0), stop=(fc == 7))
                ins.then_inc(G("fop"), 1)
            if FST <= 1:
                continue
            for dc in range(8):
                i = 8 * pf + dc
                t.wait_ge(G("fstq"), i + 1)
                t.matmul(pv1[0:2, 0:512], ones2[:, :], f_xsq[:, 0:512],
                         start=(dc == 0), stop=(dc == 7)).then_inc(G("fsq2"), 1)
                t.wait_ge(G("fx1"), i + 1)
                ins = t.matmul(pv0[0:2, 0:512], ones2[:, :], f_x1p[:, dc * 512:(dc + 1) * 512],
                               start=(dc == 0), stop=(dc == 7))
            ins.then_inc(G("fstm"), 1)
            if FSB <= 2:
                continue
            t.wait_ge(G("frst"), pf + 1)
            t.matmul(pv0[:, 0:512], onesrow[0:1, :], f_st[0:1, :], start=True, stop=True)
            t.matmul(pv1[:, 0:512], onesrow[0:1, :], f_st3[0:1, :],
                     start=True, stop=True).then_inc(G("fbc"), 1)
            if FST <= 2:
                continue
            t.wait_ge(G("fxh2"), 16 * (pf + 1))
            for fc in range(32):
                i = 32 * pf + fc
                t.wait_ge(G("fw1a" if i % 2 == 0 else "fw1b"), 16 * (i // 2 + 1))
                _w(t, G("fhr"), i - 1)
                for dc in range(8):
                    ins = t.matmul(MM[fc % 2][:, 0:512],
                                   f_w1[i % 2][:, dc * 128:(dc + 1) * 128],
                                   f_xh2[:, dc * 512:(dc + 1) * 512],
                                   start=(dc == 0), stop=(dc == 7))
                ins.then_inc(G("fhmm"), 1)
            if FST <= 3:
                continue
            t.wait_ge(G("fhq"), 32 * (pf + 1))
            for dc in range(8):
                i = 8 * pf + dc
                t.wait_ge(G("fw2a" if i % 2 == 0 else "fw2b"), 16 * (i // 2 + 1))
                _w(t, G("fspk"), i - 1)
                for fc in range(32):
                    ins = t.matmul(PVB[dc % 2][:, 0:512],
                                   f_w2[i % 2][:, fc * 128:(fc + 1) * 128],
                                   f_h[:, fc * 512:(fc + 1) * 512],
                                   start=(fc == 0), stop=(fc == 31))
                ins.then_inc(G("fyS"), 1)
                if FST <= 4:
                    continue
                if dc > 0:
                    _w(t, G("fof"), i)
                    emit_oT(i - 1)
            if FST > 4:
                _w(t, G("fof"), 8 * pf + 8)
                emit_oT(8 * pf + 7)

    @block.scalar
    def _(sc):
        scn = [0]
        slast = [None]

        def sstep(ins, sem=None, val=None):
            if sem is None:
                ins.then_inc(G("sch"), 1)
                scn[0] += 1
                slast[0] = ("sch", scn[0])
            else:
                slast[0] = (sem, val)

        def swait():
            if slast[0] is not None:
                _w(sc, G(slast[0][0]), slast[0][1])

        c_tanh = consts[:, 0:1]
        c_hpi = consts[:, 1:2]
        c_m65 = consts[:, 2:3]
        c_mh = consts[:, 3:4]
        c_eps = consts[:, 4:5]
        c_p80 = consts[:, 5:6]
        c_b40 = consts[:, 6:7]

        sc.wait_ge(G("s_dve"), 1)
        sc.wait_ge(G("s_small"), 16 * NSMALL)

        for pp in range(8):
            xst = p_xst[pp % 2]
            for tt in range(4):
                gi = 4 * pp + tt
                sc.wait_ge(G("p0x0" if pp % 2 == 0 else "p0x1"), 16 * (pp // 2 + 1))
                _w(sc, G("p0var"), gi)
                sc.activation(out=p_sqd[:, :], in_=xst[:, tt * 1024:(tt + 1) * 1024],
                              func=AF.Square, accum_out=p_stat[:, 1:2]).then_inc(G("p0sq"), 1)
                _w(sc, G("p0var"), gi + 1)
                _w(sc, G("p0xh"), gi)
                swait()
                sstep(sc.activation(out=p_stat[:, 5:6], in_=p_stat[:, 6:7], func=AF.Ln,
                                    bias=c_eps, scale=1.0))
                swait()
                sstep(sc.activation(out=p_stat[:, 7:8], in_=p_stat[:, 5:6], func=AF.Exp,
                              scale=c_mh).then_inc(G("p0rstd"), 1), "p0rstd", gi + 1)
            for gr in range(8):
                gg = 8 * pp + gr
                _w(sc, G("p0yb"), 8 * pp + gr + 1)
                sc.activation(out=p_ysq[:, :], in_=p_ybuf[:, :],
                              func=AF.Square).then_inc(G("p0ysq"), 1)
                _w(sc, G("p0nrmm"), gg + 1)
                sc.activation(out=p_nrm[0:2, :], in_=MM[gr % 2][0:2, 0:512],
                              func=AF.Sqrt).then_inc(G("p0sqr"), 1)

        sc.wait_ge(G("p0sqr"), 64)
        sc.sem_inc(G("bar"), 1)
        sc.wait_ge(G("bar"), 5)
        if upto == "p0":
            return

        for kk in range(NCH):
            h, kc = divmod(kk, NKC)
            B0 = a_B0[kk % 2]
            H0 = slice(0, 2048)
            H1 = slice(2048, 4096)
            sc.wait_ge(G("clip"), 4 * kk + 2)
            _w(sc, G("dS"), 2 * kk)
            swait()
            sstep(sc.activation(out=a_B1[:, H0], in_=B0[:, H0], func=AF.Ln, scale=-1.0, bias=1.0))
            sstep(sc.activation(out=a_B2[:, H0], in_=B0[:, H0], func=AF.Ln, scale=1.0,
                                bias=1.0).then_inc(G("lnS"), 1), "lnS", 2 * kk + 1)
            sc.wait_ge(G("clip"), 4 * kk + 4)
            sstep(sc.activation(out=a_B1[:, H1], in_=B0[:, H1], func=AF.Ln, scale=-1.0, bias=1.0))
            sstep(sc.activation(out=a_B2[:, H1], in_=B0[:, H1], func=AF.Ln, scale=1.0,
                                bias=1.0).then_inc(G("lnS"), 1), "lnS", 2 * kk + 2)
            sc.wait_ge(G("dS"), 2 * kk + 1)
            sstep(sc.activation(out=a_B1[:, H0], in_=B0[:, H0], func=AF.Tanh, scale=c_tanh))
            swait()
            sstep(sc.activation(out=a_B2[:, H0], in_=a_B1[:, H0], func=AF.Arctan))
            swait()
            sstep(sc.activation(out=a_B1[:, H0], in_=a_B2[:, H0], func=AF.Ln, scale=-2.0, bias=c_hpi))
            sc.wait_ge(G("dS"), 2 * kk + 2)
            swait()
            sstep(sc.activation(out=a_B1[:, H1], in_=B0[:, H1], func=AF.Tanh, scale=c_tanh))
            swait()
            sstep(sc.activation(out=a_B2[:, H1], in_=a_B1[:, H1], func=AF.Arctan))
            swait()
            sstep(sc.activation(out=a_B1[:, H1], in_=a_B2[:, H1], func=AF.Ln, scale=-2.0, bias=c_hpi))
            _w(sc, G("pvS"), 8 * kk)
            swait()
            sstep(sc.activation(out=a_s[:, :], in_=a_B1[:, :], func=AF.Exp, scale=c_m65,
                                accum_out=a_ncc[:, 0:1]).then_inc(G("sS"), 1), "sS", kk + 1)
            _w(sc, G("vtS"), kk)
            swait()
            sstep(sc.activation(out=a_ncc[:, 1:2], in_=a_ncc[:, 0:1], func=AF.Ln,
                                scale=c_p80))
            swait()
            sstep(sc.activation(out=a_ncc[:, 2:3], in_=a_ncc[:, 1:2], func=AF.Exp,
                          scale=c_mh, bias=c_b40).then_inc(G("cS"), 1), "cS", kk + 1)
            if kc == NKC - 1:
                for ppn in range(8):
                    e = 8 * h + ppn
                    sc.wait_ge(G("stg"), 32 * (e + 1))
                    swait()
                    sstep(sc.activation(out=a_sa[0:1, :], in_=a_nr[0:1, :], func=AF.Ln,
                                        scale=c_p80[0:1, :]))
                    swait()
                    sstep(sc.activation(out=a_sb[0:1, :], in_=a_sa[0:1, :], func=AF.Exp,
                                  scale=c_mh[0:1, :], bias=c_b40[0:1, :]).then_inc(G("ep1"), 1), "ep1", e + 1)

        sc.wait_ge(G("ep1"), 64)
        sc.wait_ge(G("cS"), NCH)
        sc.sem_inc(G("bar"), 1)
        if upto in ("attn", "exch"):
            return
        sc.wait_ge(G("bar"), 10)
        if upto in ("gat", "ffn0d", "ffn0e", "ffn0f", "ffn0g", "ffn0m", "ffn0n", "ffn0p", "ffn0q", "ffn0r", "ffn0u", "ffn0u2", "ffn0u3"):
            return

        if FST <= 1:
            return
        for pf in range(NPF):
            for dc in range(8):
                i = 8 * pf + dc
                sc.wait_ge(G("fx1"), i + 1)
                _w(sc, G("fsq2"), i)
                sc.activation(out=f_xsq[:, :], in_=f_x1p[:, dc * 512:(dc + 1) * 512],
                              func=AF.Square).then_inc(G("fstq"), 1)
            if FSB <= 1 or FSB in (15, 16, 17):
                continue
            sc.wait_ge(G("fvar"), pf + 1)
            sstep(sc.activation(out=f_st2[0:1, :], in_=f_st4[0:1, :], func=AF.Ln,
                                bias=c_eps[0:1, :], scale=1.0))
            swait()
            sc.activation(out=f_st3[0:1, :], in_=f_st2[0:1, :], func=AF.Exp,
                          scale=c_mh[0:1, :]).then_inc(G("frst"), 1)
            if FST <= 2:
                continue
            for fc in range(32):
                i = 32 * pf + fc
                sc.wait_ge(G("fhr"), i + 1)
                if pf > 0 and fc == 0:
                    sc.wait_ge(G("fyS"), 8 * pf)
                sc.activation(out=f_h[:, fc * 512:(fc + 1) * 512], in_=f_hr[i % 2][:, :],
                              func=AF.Square).then_inc(G("fhq"), 1)

    @block.vector
    def _(v):
        vcn = [0]
        vlast = [None]

        def vstep(ins, sem=None, val=None):
            if sem is None:
                ins.then_inc(G("vch"), 1)
                vcn[0] += 1
                vlast[0] = ("vch", vcn[0])
            else:
                vlast[0] = (sem, val)

        def vwait():
            if vlast[0] is not None:
                _w(v, G(vlast[0][0]), vlast[0][1])

        v.memset(consts[:, 0:1], 0.25)
        v.memset(consts[:, 1:2], HPI1)
        v.memset(consts[:, 2:3], -65.0)
        v.memset(consts[:, 3:4], -0.5)
        v.memset(consts[:, 4:5], 1e-5)
        v.memset(consts[:, 5:6], 1.2089258196146292e24)
        v.memset(consts[:, 6:7], 27.725887222397812)
        v.memset(onesrow[:, :], 1.0)
        v.memset(onescol[:, :], 1.0)
        v.memset(ones2[:, :], 1.0)
        v.memset(a_vtt[0][:, 65:66], 1.0)
        v.memset(a_vtt[1][:, 65:66], 1.0).then_inc(G("s_dve"), 1)
        v.wait_ge(G("p0bc"), 1)
        v.tensor_copy(out=bvbc[:, :], in_=mm0[:, 0:512]).then_inc(G("s_bv"), 1)

        for i in range(24):
            wsb = [p_wqsb, p_wksb, p_wvsb][i // 8]
            dc = i % 8
            v.wait_ge(G("p0wld0" if i % 2 == 0 else "p0wld1"), 16 * (i // 2 + 1))
            v.tensor_copy(out=wsb[:, dc * 512:(dc + 1) * 512],
                          in_=p_wst[i % 2][:, :]).then_inc(G("p0wc"), 1)
        for pp in range(8):
            xst = p_xst[pp % 2]
            _w(v, G("p0mm"), 12 * pp)
            for tt in range(4):
                gi = 4 * pp + tt
                v.wait_ge(G("p0x0" if pp % 2 == 0 else "p0x1"), 16 * (pp // 2 + 1))
                vwait()
                vstep(v.reduce_sum(out=p_stat[:, 0:1], in_=xst[:, tt * 1024:(tt + 1) * 1024],
                                   axis=mybir.AxisListType.X))
                v.wait_ge(G("p0sq"), gi + 1)
                vwait()
                vstep(v.tensor_scalar(out=p_stat[:, 0:1], in0=p_stat[:, 0:1],
                                      scalar1=1.0 / D, scalar2=None, op0=ALU.mult))
                vstep(v.tensor_scalar(out=p_stat[:, 1:2], in0=p_stat[:, 1:2],
                                      scalar1=1.0 / D, scalar2=None, op0=ALU.mult))
                vwait()
                vstep(v.tensor_tensor(out=p_stat[:, 2:3], in0=p_stat[:, 0:1],
                                      in1=p_stat[:, 0:1], op=ALU.mult))
                vwait()
                v.tensor_tensor(out=p_stat[:, 6:7], in0=p_stat[:, 1:2],
                                in1=p_stat[:, 2:3], op=ALU.subtract).then_inc(G("p0var"), 1)
                v.wait_ge(G("p0rstd"), gi + 1)
                if tt == 0:
                    _w(v, G("p0T"), 32 * pp)
                vstep(v.tensor_scalar(out=p_xhat[:, tt * 1024:(tt + 1) * 1024],
                                      in0=xst[:, tt * 1024:(tt + 1) * 1024],
                                      scalar1=p_stat[:, 0:1], scalar2=p_stat[:, 7:8],
                                      op0=ALU.subtract, op1=ALU.mult).then_inc(G("p0xh"), 1), "p0xh", gi + 1)
            for i8 in range(32):
                i = 32 * pp + i8
                tt = i8 // 8
                j = i8 % 8
                v.wait_ge(G("p0T"), i + 1)
                v.tensor_copy(out=p_xhT[:, j * 512 + tt * 128: j * 512 + (tt + 1) * 128],
                              in_=MM[i % 2][:, 0:128]).then_inc(G("p0Tev"), 1)
            for gr in range(12):
                gg = 12 * pp + gr
                si = gg
                stage = p_stage[si % 2]
                v.wait_ge(G("p0mm"), gg + 1)
                if gr < 8:
                    e = 8 * pp + gr
                    bias = (bq_sb if gr < 4 else bk_sb)[:, (gr % 4):(gr % 4) + 1]
                    vwait()
                    vstep(v.tensor_scalar(out=p_ybuf[:, :], in0=PVB[gr % 2][:, 0:512],
                                          scalar1=bias, scalar2=None,
                                          op0=ALU.add).then_inc(G("p0yb"), 1), "p0yb", 8 * pp + gr + 1)
                    v.wait_ge(G("p0sqr"), e + 1)
                    vstep(v.tensor_scalar(out=p_nrm[0:2, :], in0=p_nrm[0:2, :],
                                          scalar1=1e-12, scalar2=None, op0=ALU.max))
                    vwait()
                    v.reciprocal(out=p_inv[0:2, :], in_=p_nrm[0:2, :]).then_inc(G("p0inv"), 1)
                    v.wait_ge(G("p0bc"), e + 2)
                    if si >= 2:
                        v.wait_ge(G("p0sd0" if si % 2 == 0 else "p0sd1"), 16 * (si // 2))
                    vwait()
                    vstep(v.tensor_tensor(out=stage[:, :], in0=p_ybuf[:, :],
                                          in1=MM[gr % 2][:, 0:512],
                                          op=ALU.mult).then_inc(G("p0st"), 1), "p0st", si + 1)
                else:
                    v.wait_ge(G("s_bv"), 1)
                    if si >= 2:
                        v.wait_ge(G("p0sd0" if si % 2 == 0 else "p0sd1"), 16 * (si // 2))
                    vstep(v.tensor_tensor(out=stage[:, :], in0=PVB[gr % 2][:, 0:512],
                                          in1=bvbc[:, :],
                                          op=ALU.add).then_inc(G("p0st"), 1), "p0st", si + 1)

        v.wait_ge(G("p0st"), 96)
        v.sem_inc(G("bar"), 1)
        v.wait_ge(G("bar"), 5)
        if upto == "p0":
            return

        def emit_clip(kk):
            B0 = a_B0[kk % 2]
            _w(v, G("sS"), kk - 1)
            for q in range(4):
                Q = 4 * kk + q
                v.wait_ge(G("dot"), Q + 1)
                pb = psA if q % 2 == 0 else psB
                v.tensor_scalar(out=B0[:, q * 1024:(q + 1) * 1024], in0=pb[:, :],
                                scalar1=CLIPV, scalar2=-CLIPV,
                                op0=ALU.min, op1=ALU.max).then_inc(G("clip"), 1)

        emit_clip(0)
        for kk in range(NCH):
            h, kc = divmod(kk, NKC)
            B0 = a_B0[kk % 2]
            v.wait_ge(G("lnS"), 2 * kk + 1)
            v.tensor_tensor(out=B0[:, 0:2048], in0=a_B2[:, 0:2048], in1=a_B1[:, 0:2048],
                            op=ALU.subtract).then_inc(G("dS"), 1)
            v.wait_ge(G("lnS"), 2 * kk + 2)
            v.tensor_tensor(out=B0[:, 2048:4096], in0=a_B2[:, 2048:4096], in1=a_B1[:, 2048:4096],
                            op=ALU.subtract).then_inc(G("dS"), 1)
            if kk + 1 < NCH:
                emit_clip(kk + 1)
            if kk < NWT:
                i = kk
                v.wait_ge(G("wcld0" if i % 2 == 0 else "wcld1"), 16 * (i // 2 + 1))
                if i < NWO:
                    kind, r, c, _src = cast_tiles[i]
                    dst = wo_sb[:, r * 1024 + c * 512: r * 1024 + (c + 1) * 512]
                else:
                    io = i - NWO
                    if io >= 2:
                        v.wait_ge(G("wcod0" if io % 2 == 0 else "wcod1"), 16 * (io // 2))
                    dst = a_wb[i % 2][:, :]
                v.tensor_copy(out=dst, in_=a_wc[i % 2][:, :]).then_inc(G("wcdn"), 1)
                if i == NWO - 1:
                    v.sem_inc(G("wcst"), 1)
            v.wait_ge(G("cS"), kk + 1)
            v.wait_ge(G("hload0" if h % 2 == 0 else "hload1"), 48 * (h // 2 + 1))
            _w(v, G("pvS"), 8 * (kk - 1))
            vt_t = a_vtt[kk % 2]
            v.tensor_scalar(out=vt_t[:, 0:64], in0=a_vt[h % 2][:, kc * 64:(kc + 1) * 64],
                            scalar1=a_ncc[:, 2:3], scalar2=None, op0=ALU.mult)
            v.tensor_copy(out=vt_t[:, 64:65], in_=a_ncc[:, 2:3]).then_inc(G("vtS"), 1)
            for p in range(8):
                v.wait_ge(G("pvS"), 8 * kk + p + 1)
                if kc == 0 and p == 0:
                    _w(v, G("attS"), 8 * h)
                pvb = PVB[(8 * kk + p) % 2]
                if kc == 0:
                    ins = v.tensor_copy(out=a_hacc[0:66, p * 512:(p + 1) * 512],
                                        in_=pvb[0:66, 0:512])
                else:
                    ins = v.tensor_tensor(out=a_hacc[0:66, p * 512:(p + 1) * 512],
                                          in0=pvb[0:66, 0:512],
                                          in1=a_hacc[0:66, p * 512:(p + 1) * 512],
                                          op=ALU.add)
                ins.then_inc(G("evS"), 1)
            if kc == NKC - 1:
                for ppn in range(8):
                    e = 8 * h + ppn
                    v.wait_ge(G("ep1"), e + 1)
                    vstep(v.tensor_tensor(out=a_sc[0:1, :], in0=a_sb[0:1, :],
                                          in1=a_tq[0:1, :], op=ALU.mult))
                    vwait()
                    vstep(v.tensor_scalar(out=a_sd[0:1, :], in0=a_sc[0:1, :],
                                          scalar1=1e-12, scalar2=None, op0=ALU.max))
                    vwait()
                    vstep(v.reciprocal(out=a_sc[0:1, :], in_=a_sd[0:1, :]))
                    vwait()
                    v.tensor_tensor(out=a_sd[0:1, :], in0=a_sb[0:1, :],
                                    in1=a_sc[0:1, :], op=ALU.mult).then_inc(G("scl"), 1)
                    v.wait_ge(G("bcS"), e + 1)
                    row0 = 64 * (h % 2)
                    cbase = (h // 2) * S
                    v.tensor_tensor(out=attn_all[row0:row0 + 64,
                                                 cbase + ppn * 512: cbase + (ppn + 1) * 512],
                                    in0=a_hacc[0:64, ppn * 512:(ppn + 1) * 512],
                                    in1=MM[ppn % 2][0:64, 0:512],
                                    op=ALU.mult).then_inc(G("attS"), 1)

        v.wait_ge(G("attS"), 64)
        v.wait_ge(G("evS"), 8 * NCH)
        v.sem_inc(G("bar"), 1)
        if upto in ("attn", "exch"):
            return
        v.wait_ge(G("bar"), 10)
        if upto in ("ffn0p", "ffn0q", "ffn0r", "ffn0u", "ffn0u2", "ffn0u3"):
            nT = {"ffn0p": 1, "ffn0q": 2}.get(upto, 8)
            HS = upto.startswith("ffn0u")
            for i in range(nT):
                v.wait_ge(G("fxT"), i + 1)
                v.tensor_copy(out=f_xfm[:, i * 128:(i + 1) * 128],
                              in_=MM[i % 2][:, 0:128]).then_inc(G("fxfm"), 1)
            return
        if upto in ("ffn0m", "ffn0n"):
            return
        if upto == "ffn0g":
            v.wait_ge(G("fxld"), 16)
            v.tensor_copy(out=f_xfm[:, 0:128], in_=MM[0][:, 0:128])
            return
        if upto in ("gat", "ffn0b", "ffn0d", "ffn0e", "ffn0f"):
            return

        for pf in range(NPF):
            for i8 in range(8 * NTT):
                i = 32 * pf + i8
                tt = i8 // 8
                j = i8 % 8
                v.wait_ge(G("fxT"), i + 1)
                v.tensor_copy(out=f_xfm[:, j * 512 + tt * 128: j * 512 + (tt + 1) * 128],
                              in_=MM[i % 2][:, 0:128]).then_inc(G("fxfm"), 1)
            if FST <= 0:
                continue
            for dc in range(8):
                i = 8 * pf + dc
                v.wait_ge(G("fop"), i + 1)
                if dc == 0 and FST > 4:
                    _w(v, G("fof"), 8 * pf)
                v.scalar_tensor_tensor(out=f_x1p[:, dc * 512:(dc + 1) * 512],
                                       in0=MM[dc % 2][:, 0:512],
                                       scalar=bo_sb[:, dc:dc + 1],
                                       in1=f_xfm[:, dc * 512:(dc + 1) * 512],
                                       op0=ALU.add, op1=ALU.add).then_inc(G("fx1"), 1)
            if FST <= 1:
                continue
            if FSB <= 1:
                continue
            v.wait_ge(G("fstm"), pf + 1)
            vstep(v.tensor_scalar(out=f_st[0:1, :], in0=pv0[0:1, 0:512],
                                  scalar1=1.0 / D, scalar2=None, op0=ALU.mult))
            vstep(v.tensor_scalar(out=f_st2[0:1, :], in0=pv1[0:1, 0:512],
                                  scalar1=1.0 / D, scalar2=None, op0=ALU.mult))
            vwait()
            vstep(v.tensor_tensor(out=f_st3[0:1, :], in0=f_st[0:1, :], in1=f_st[0:1, :],
                                  op=ALU.mult))
            vwait()
            v.tensor_tensor(out=f_st4[0:1, :], in0=f_st2[0:1, :], in1=f_st3[0:1, :],
                            op=ALU.subtract).then_inc(G("fvar"), 1)
            if FSB <= 3:
                continue
            v.wait_ge(G("fbc"), pf + 1)
            for dc in range(8):
                if pf > 0 and dc == 0:
                    v.wait_ge(G("fhmm"), 32 * pf)
                vstep(v.tensor_tensor(out=f_y[:, :], in0=f_x1p[:, dc * 512:(dc + 1) * 512],
                                      in1=pv0[:, 0:512], op=ALU.subtract))
                vwait()
                v.tensor_tensor(out=f_xh2[:, dc * 512:(dc + 1) * 512], in0=f_y[:, :],
                                in1=pv1[:, 0:512], op=ALU.mult).then_inc(G("fxh2"), 2)
            if FST <= 2:
                continue
            for fc in range(32):
                i = 32 * pf + fc
                v.wait_ge(G("fhmm"), i + 1)
                _w(v, G("fhq"), i - 1)
                v.tensor_scalar(out=f_hr[i % 2][:, :], in0=MM[fc % 2][:, 0:512],
                                scalar1=0.0, scalar2=None, op0=ALU.max).then_inc(G("fhr"), 1)
            if FST <= 3:
                continue
            for dc in range(8):
                i = 8 * pf + dc
                for dcp in ([dc - 1] if dc > 0 and FST > 4 else []):
                    ip = 8 * pf + dcp
                    for tc in range(4):
                        j = 4 * ip + tc
                        v.wait_ge(G("foT"), j + 1)
                        if j >= 4:
                            v.wait_ge(G("fout0" if j % 2 == 0 else "fout1"), 16 * (j // 2 - 1))
                        v.tensor_copy(out=f_os[j % 4][:, :],
                                      in_=MM[j % 2][:, 0:128]).then_inc(G("foev"), 1)
                v.wait_ge(G("fyS"), i + 1)
                pvb = PVB[dc % 2]
                vwait()
                vstep(v.tensor_copy(out=f_y[:, :], in_=pvb[:, 0:512]).then_inc(G("fspk"), 1), "fspk", i + 1)
                vwait()
                vstep(v.tensor_scalar(out=f_spk[:, :], in0=f_y[:, :], scalar1=1.0,
                                      scalar2=None, op0=ALU.is_gt))
                vwait()
                vstep(v.tensor_copy(out=f_acc[:, :], in_=f_spk[:, :]))
                cur = f_y
                bufs = [f_mA, f_mB, f_mC]
                bi = 0
                for st in range(4):
                    tmp = bufs[bi % 3]
                    nxt = bufs[(bi + 1) % 3]
                    bi += 2
                    vwait()
                    vstep(v.scalar_tensor_tensor(out=tmp[:, :], in0=cur[:, :], scalar=0.9,
                                                 in1=f_y[:, :], op0=ALU.mult, op1=ALU.add))
                    vwait()
                    vstep(v.tensor_tensor(out=nxt[:, :], in0=tmp[:, :], in1=f_spk[:, :],
                                          op=ALU.subtract))
                    vwait()
                    vstep(v.tensor_scalar(out=f_spk[:, :], in0=nxt[:, :], scalar1=1.0,
                                          scalar2=None, op0=ALU.is_gt))
                    vwait()
                    vstep(v.tensor_tensor(out=f_acc[:, :], in0=f_acc[:, :], in1=f_spk[:, :],
                                          op=ALU.add))
                    cur = nxt
                if FST > 4:
                    _w(v, G("foT"), 4 * i)
                vwait()
                v.scalar_tensor_tensor(out=f_ofm[:, :], in0=f_acc[:, :], scalar=0.2,
                                       in1=f_x1p[:, dc * 512:(dc + 1) * 512],
                                       op0=ALU.mult, op1=ALU.add).then_inc(G("fof"), 1)
            for dcp in ([7] if FST > 4 else []):
                ip = 8 * pf + dcp
                for tc in range(4):
                    j = 4 * ip + tc
                    v.wait_ge(G("foT"), j + 1)
                    if j >= 4:
                        v.wait_ge(G("fout0" if j % 2 == 0 else "fout1"), 16 * (j // 2 - 1))
                    v.tensor_copy(out=f_os[j % 4][:, :],
                                  in_=MM[j % 2][:, 0:128]).then_inc(G("foev"), 1)

    @block.sync
    def _(sp):
        loads = [
            (ident[:, :], id_in), (eye2[:, :], eye2_in), (blk2[:, :], blk2_in),
            (bq_sb[:, :], bq_in), (bk_sb[:, :], bk_in), (bo_sb[:, :], bo_in),
            (bvrow[:, :], bv_in), (sidx[:, :], sidx_in), (gidx[:, :], gidx_in),
            (fme[:, :], fme_in), (fpr[:, :], fpr_in), (nonce[:, :], non_in),
        ]
        assert len(loads) == NSMALL
        for dst, src in loads:
            sp.dma_start(out=dst, in_=src).then_inc(G("s_small"), 16)

        for i in range(24):
            w_in = [wq_in, wk_in, wv_in][i // 8]
            dc = i % 8
            _w(sp, G("p0wc"), i - 1)
            sp.dma_start(out=p_wst[i % 2][:, :],
                         in_=w_in[dc * 128:(dc + 1) * 128, :]).then_inc(G("p0wld0" if i % 2 == 0 else "p0wld1"), 16)
        def xload(pp):
            _w(sp, G("p0xh"), 4 * (pp - 1))
            sp.dma_start(out=p_xst[pp % 2][:, :].rearrange("p (t d) -> p t d", t=4),
                         in_=x_in[pp * 512:(pp + 1) * 512, :].rearrange("(t p) d -> p t d", p=128)
                         ).then_inc(G("p0x0" if pp % 2 == 0 else "p0x1"), 16)

        xload(0)
        xload(1)
        for pp in range(8):
            for gr in range(12):
                si = 12 * pp + gr
                stage = p_stage[si % 2]
                sp.wait_ge(G("p0st"), si + 1)
                if gr < 8:
                    dstt = qs if gr < 4 else ks
                    fc = gr % 4
                    ins = sp.dma_start(out=dstt.ap()[fc * 128:(fc + 1) * 128,
                                                     pp * 512:(pp + 1) * 512],
                                       in_=stage[:, :])
                else:
                    kcg = 4 * pp + (gr - 8)
                    ins = sp.dma_start(out=vts.ap()[kcg, :, :], in_=stage[:, :])
                ins.then_inc(G("p0sd0" if si % 2 == 0 else "p0sd1"), 16)
            if pp + 2 < 8:
                xload(pp + 2)
        sp.wait_ge(G("p0sd0"), 16 * 48)
        sp.wait_ge(G("p0sd1"), 16 * 48)
        sp.sem_inc(G("bar"), 1)
        sp.wait_ge(G("bar"), 5)
        if upto == "p0":
            return

        def head_load(h):
            hs = "hload0" if h % 2 == 0 else "hload1"
            sp.dma_start(out=a_qh[h % 2][:, :],
                         in_=qs.ap()[h * 64:(h + 1) * 64, :]).then_inc(G(hs), 16)
            sp.dma_start(out=a_kh[h % 2][:, :],
                         in_=ks.ap()[h * 64:(h + 1) * 64, :]).then_inc(G(hs), 16)
            sp.dma_start(out=a_vt[h % 2][:, :].rearrange("p (k f) -> p k f", k=NKC),
                         in_=vts.ap()[:, :, h * 64:(h + 1) * 64].rearrange("k p f -> p k f")
                         ).then_inc(G(hs), 16)

        head_load(0)
        head_load(1)
        for h in range(HL):
            sp.wait_ge(G("evS"), 256 * (h + 1))
            for ppn in range(8):
                e = 8 * h + ppn
                _w(sp, G("ep1"), e)
                _w(sp, G("scl"), e)
                sp.dma_start(out=a_nr[0:1, :],
                             in_=a_hacc[65:66, ppn * 512:(ppn + 1) * 512]).then_inc(G("stg"), 16)
                sp.dma_start(out=a_tq[0:1, :],
                             in_=a_hacc[64:65, ppn * 512:(ppn + 1) * 512]).then_inc(G("stg"), 16)
            if h + 2 < HL:
                sp.wait_ge(G("dot"), 128 * (h + 1))
                sp.wait_ge(G("vtS"), 32 * (h + 1))
                head_load(h + 2)
        if upto == "dump":
            sp.wait_ge(G("attS"), 64)
            sp.dma_start(out=dbg_q, in_=qs.ap()).then_inc(G("fout0"), 16)
            sp.dma_start(out=dbg_k, in_=ks.ap()).then_inc(G("fout0"), 16)
            sp.dma_start(out=dbg_a, in_=attn_all[:, :]).then_inc(G("fout0"), 16)
            sp.dma_start(out=dbg_h, in_=a_hacc[:, :]).then_inc(G("fout0"), 16)
            sp.dma_start(out=dbg_u, in_=a_B1[:, :]).then_inc(G("fout0"), 16)
            sp.dma_start(out=dbg_s, in_=a_s[:, :]).then_inc(G("fout0"), 16)
            sp.dma_start(out=dbg_n, in_=a_ncc[:, :]).then_inc(G("fout0"), 16)
            sp.wait_ge(G("fout0"), 112)
        sp.wait_ge(G("stg"), 32 * 64)
        sp.sem_inc(G("bar"), 1)
        if upto in ("attn", "exch"):
            return
        sp.wait_ge(G("bar"), 10)
        if upto == "gat":
            return
        if upto == "ffn0e":
            sp.dma_start(out=fval[:, :], in_=non_in).then_inc(G("fxld"), 16)
            sp.wait_ge(G("fxld"), 16)
            return
        if upto in ("ffn0f", "ffn0g"):
            sp.dma_start(out=f_x[:, :].rearrange("p (t d) -> p t d", t=4),
                         in_=x_in[0:512, :].rearrange("(t p) d -> p t d", p=128)
                         ).then_inc(G("fxld"), 16)
            sp.wait_ge(G("fxld"), 16)
            return
        if upto == "ffn0d":
            return

        for pf in range(NPF):
            _w(sp, G("fxT"), 32 * pf)
            if upto == "ffn0i":
                sp.sem_inc(G("fxld"), 16)
                continue
            sp.dma_start(out=f_x[:, :].rearrange("p (t d) -> p t d", t=4),
                         in_=x_in[pf * 512:(pf + 1) * 512, :].rearrange("(t p) d -> p t d", p=128)
                         ).then_inc(G("fxld"), 16)
            if FST <= 4:
                continue
            for dc in range(8):
                for tc in range(4):
                    j = 32 * pf + 4 * dc + tc
                    sp.wait_ge(G("foev"), j + 1)
                    sp.dma_start(out=out_o[pf * 512 + tc * 128: pf * 512 + (tc + 1) * 128,
                                           dc * 128:(dc + 1) * 128],
                                 in_=f_os[j % 4][:, :]).then_inc(G("fout0" if j % 2 == 0 else "fout1"), 16)
        if FST > 4:
            sp.wait_ge(G("fout0"), 16 * 16 * NPF)
            sp.wait_ge(G("fout1"), 16 * 16 * NPF)

    @block.gpsimd
    def _(gp):
        gp.sem_inc(G("bar"), 1)
        gp.wait_ge(G("bar"), 5)
        if upto == "p0":
            return
        for i in range(NWT):
            kind, r, c, src = cast_tiles[i]
            if i >= 2:
                gp.wait_ge(G("wcdn"), i - 1)
            gp.dma_start(out=a_wc[i % 2][:, :], in_=src).then_inc(G("wcld0" if i % 2 == 0 else "wcld1"), 16)
            iw = i - 1
            if iw >= NWO:
                kindw, rw, cw, _s = cast_tiles[iw]
                dstt = wfcb if kindw == "wfc" else wpjb
                gp.wait_ge(G("wcdn"), iw + 1)
                oi = iw - NWO
                gp.dma_start(out=dstt.ap()[rw * 128:(rw + 1) * 128, cw * 512:(cw + 1) * 512],
                             in_=a_wb[iw % 2][:, :]).then_inc(G("wcod0" if oi % 2 == 0 else "wcod1"), 16)
        iw = NWT - 1
        kindw, rw, cw, _s = cast_tiles[iw]
        gp.wait_ge(G("wcdn"), iw + 1)
        oi = iw - NWO
        gp.dma_start(out=(wfcb if kindw == "wfc" else wpjb).ap()[rw * 128:(rw + 1) * 128,
                                                                cw * 512:(cw + 1) * 512],
                     in_=a_wb[iw % 2][:, :]).then_inc(G("wcod0" if oi % 2 == 0 else "wcod1"), 16)
        if upto == "attn":
            gp.sem_inc(G("bar"), 1)
            return
        for h in range(HL):
            gp.wait_ge(G("attS"), 8 * (h + 1))
            row0 = 64 * (h % 2)
            cbase = (h // 2) * S
            gp.indirect_dma_start(
                shd.ap(), bass.IndirectOffsetOnAxis(sidx[:, h:h + 1], 0),
                attn_all[row0:row0 + 64, cbase + TM: cbase + S], None,
            ).then_inc(G("exs"), 16)
        gp.wait_ge(G("exs"), 16 * HL)
        gp.wait_ge(G("wcod0"), 16 * ((NWT - NWO) // 2))
        gp.wait_ge(G("wcod1"), 16 * ((NWT - NWO) // 2))
        gp.indirect_dma_start(
            shf.ap(), bass.IndirectOffsetOnAxis(fme[:, 0:1], 0), nonce[:, :], None,
        ).then_inc(G("exs"), 16)
        gp.wait_ge(G("exs"), 16 * (HL + 1))
        with gp.register("cnt") as cnt, gp.register("nr_") as nr_, gp.register("df") as df:
            nv = gp.value_load(nonce[0:1, 0:1])
            gp.reg_mov(nr_, nv)
            gp.reg_mov(cnt, 0)

            def cond():
                gp.reg_add(cnt, cnt, 16)
                gp.indirect_dma_start(
                    fval[:, :], None, shf.ap(),
                    bass.IndirectOffsetOnAxis(fpr[:, 0:1], 0),
                ).then_inc(G("pls"), 16)
                gp.wait_ge(G("pls"), cnt)
                vv = gp.value_load(fval[0:1, 0:1])
                gp.reg_sub(df, vv, nr_)
                return df

            with gp.While(cond):
                gp.nop()
        gp.sem_inc(G("bar"), 1)
        if upto == "exch":
            return
        gp.wait_ge(G("bar"), 10)
        if upto in ("ffn0f", "ffn0g", "ffn0m", "ffn0n", "ffn0p", "ffn0q", "ffn0r", "ffn0u", "ffn0u2", "ffn0u3"):
            return
        if upto == "ffn0d":
            gp.dma_start(out=f_x[:, :].rearrange("p (t d) -> p t d", t=4),
                         in_=x_in[0:512, :].rearrange("(t p) d -> p t d", p=128)
                         ).then_inc(G("fxld"), 16)
            gp.wait_ge(G("fxld"), 16)
            return
        if upto == "ffn0a":
            return
        for pf in range(2 if upto == "gat" else NPF):
            _w(gp, G("fop"), 8 * (pf - 1))
            for fc in range(4):
                gp.indirect_dma_start(
                    f_pap[pf % 2][:, fc * 512:(fc + 1) * 512], None,
                    shd.ap(),
                    bass.IndirectOffsetOnAxis(gidx[:, fc:fc + 1], 0),
                    element_offset=pf * 512,
                ).then_inc(G("gat0" if pf % 2 == 0 else "gat1"), 16)
            if upto == "gat" or FST <= 2:
                continue
            for fc in range(32):
                i = 32 * pf + fc
                _w(gp, G("fhmm"), i - 1)
                gp.dma_start(out=f_w1[i % 2][:, :].rearrange("p (a c) -> p a c", a=8),
                             in_=wfcb.ap()[:, fc * 128:(fc + 1) * 128]
                             .rearrange("(a p) c -> p a c", p=128)).then_inc(G("fw1a" if i % 2 == 0 else "fw1b"), 16)
            if FST <= 3:
                continue
            for dc in range(8):
                i = 8 * pf + dc
                _w(gp, G("fyS"), i - 1)
                gp.dma_start(out=f_w2[i % 2][:, :].rearrange("p (a c) -> p a c", a=32),
                             in_=wpjb.ap()[:, dc * 128:(dc + 1) * 128]
                             .rearrange("(a p) c -> p a c", p=128)).then_inc(G("fw2a" if i % 2 == 0 else "fw2b"), 16)

    return nc, stack


_NC_CACHE = {}


def _get_nc():
    if "nc" not in _NC_CACHE:
        _NC_CACHE["nc"] = build_nc()
    return _NC_CACHE["nc"]


def make_in_maps(L0_attn_x, L0_attn_wq, L0_attn_bq, L0_attn_wk, L0_attn_bk,
                 L0_attn_wv, L0_attn_bv, L0_attn_wo, L0_attn_bo,
                 L0_ffn_W_fc, L0_ffn_W_proj):
    import secrets
    x = np.ascontiguousarray(np.asarray(L0_attn_x, dtype=np.float32))
    wq = np.ascontiguousarray(np.asarray(L0_attn_wq, dtype=np.float32))
    wk = np.ascontiguousarray(np.asarray(L0_attn_wk, dtype=np.float32))
    wv = np.ascontiguousarray(np.asarray(L0_attn_wv, dtype=np.float32))
    bq = np.asarray(L0_attn_bq, dtype=np.float32)
    bk = np.asarray(L0_attn_bk, dtype=np.float32)
    bv = np.asarray(L0_attn_bv, dtype=np.float32)
    wo = np.ascontiguousarray(np.asarray(L0_attn_wo, dtype=np.float32))
    bo = np.asarray(L0_attn_bo, dtype=np.float32)
    wfcT = np.ascontiguousarray(np.asarray(L0_ffn_W_fc, dtype=np.float32).T)
    wpjT = np.ascontiguousarray(np.asarray(L0_ffn_W_proj, dtype=np.float32).T)

    identity = np.eye(128, dtype=np.float32)
    eye2 = np.zeros((128, 2), dtype=np.float32)
    eye2[0:64, 0] = 1.0
    eye2[64:128, 1] = 1.0
    blk2 = np.zeros((2, 128), dtype=np.float32)
    blk2[0, 0:64] = 1.0
    blk2[1, 64:128] = 1.0
    nonce_val = (secrets.randbits(30) | 1)

    in_maps = []
    for cid in range(8):
        b = cid // 2
        hg = cid % 2
        pr = cid ^ 1
        perm = np.concatenate([np.arange(hg * TM, (hg + 1) * TM),
                               np.arange((1 - hg) * TM, (2 - hg) * TM) % S])
        fsl = slice(hg * FQ, (hg + 1) * FQ)
        wo_perm = np.concatenate([wo[hg * FQ:(hg + 1) * FQ, :],
                                  wo[(1 - hg) * FQ:(2 - hg) * FQ, :]], axis=0)
        m = {
            "x_in": np.ascontiguousarray(x[b][perm]),
            "wq_in": np.ascontiguousarray(wq[:, fsl]),
            "wk_in": np.ascontiguousarray(wk[:, fsl]),
            "wv_in": np.ascontiguousarray(wv[:, fsl]),
            "bq_in": np.ascontiguousarray(bq[fsl].reshape(4, 128).T),
            "bk_in": np.ascontiguousarray(bk[fsl].reshape(4, 128).T),
            "bv_in": np.ascontiguousarray(bv[fsl].reshape(1, FQ)),
            "wo_in": np.ascontiguousarray(wo_perm),
            "bo_in": np.ascontiguousarray(bo.reshape(8, 128).T),
            "wfc_in": wfcT,
            "wpj_in": wpjT,
            "id_in": identity,
            "eye2_in": eye2,
            "blk2_in": blk2,
            "sidx_in": np.ascontiguousarray(
                (cid * FQ + np.arange(64)[:, None] + 64 * (np.arange(8)[None, :] % 2)
                 + 128 * (np.arange(8)[None, :] // 2)).astype(np.int32)),
            "gidx_in": np.ascontiguousarray(
                (pr * FQ + np.arange(128)[:, None] + 128 * np.arange(4)[None, :]).astype(np.int32)),
            "fme_in": np.array([[2 * cid], [2 * cid + 1]], dtype=np.int32),
            "fpr_in": np.array([[2 * pr], [2 * pr + 1]], dtype=np.int32),
            "non_in": np.full((2, 16), nonce_val, dtype=np.int32),
        }
        in_maps.append(m)
    return in_maps


def assemble_out(results):
    out = np.empty((4, S, D), dtype=np.float32)
    for cid in range(8):
        b = cid // 2
        hg = cid % 2
        out[b, hg * TM:(hg + 1) * TM, :] = results[cid]["out_o"]
    return out


def kernel(**inputs):
    in_maps = make_in_maps(**inputs)
    nc, _stack = _get_nc()
    res = run_bass_kernel_spmd(nc, in_maps, core_ids=list(range(8)))
    return assemble_out([res.results[c] for c in range(8)])
